# version 24
# baseline (speedup 1.0000x reference)
import numpy as np
import concourse.bass as bass
import concourse.bass_utils as _bu
import concourse.mybir as mybir
from concourse.bass_utils import run_bass_kernel_spmd
from contextlib import ExitStack

if not getattr(_bu, "_ldw_opt_patched", False):
    _bu._ldw_opt_patched = True
    _orig_run_command = _bu.run_command

    def _run_command_ldw(argv, **kw):
        argv = [a.replace("--enable-ldw-opt=false", "--enable-ldw-opt=true")
                if isinstance(a, str) else a for a in argv]
        return _orig_run_command(argv, **kw)

    _bu.run_command = _run_command_ldw

F32 = mybir.dt.float32
BF16 = mybir.dt.bfloat16
AF = mybir.ActivationFunctionType
ALU = mybir.AluOpType
AX = mybir.AxisListType

T_CORE = 4096
D = 1024
H = 16
DK = 64
NPAIR = 8
N_TILES = 8
N_PRE = 8
N_TC = 4
EPS = 1e-5
BIG = 1.0e30

TAPS = []
MM_TAGS = []
PHASE = ["?"]
STOP_AFTER = None
SKIP_INTER = False
SKIP_SCORES = False
ENGINES = ("sync", "gpsimd", "vector", "scalar", "tensor")


_STOPPED = [False]


def _ckpt(name):
    if STOP_AFTER == name:
        _STOPPED[0] = True


class Rec:
    def __init__(self, sems):
        self.prog = {e: [] for e in ENGINES}
        self.sems = sems
        self.c = {n: 0 for n in sems}

    def op(self, eng, fn):
        if _STOPPED[0]:
            return
        self.prog[eng].append(fn)

    def inc(self, eng, fn, sem, amt=1):
        if _STOPPED[0]:
            return self.c[sem]
        h = self.sems[sem]
        self.prog[eng].append(lambda e, fn=fn, h=h, amt=amt: fn(e).then_inc(h, amt))
        self.c[sem] += amt
        return self.c[sem]

    def wait(self, eng, sem, val=None):
        if _STOPPED[0]:
            return
        if val is None:
            val = self.c[sem]
        if val <= 0:
            return
        h = self.sems[sem]
        self.prog[eng].append(lambda e, h=h, val=val: e.wait_ge(h, val))

    def emit(self, block):
        for name in ENGINES:
            prog = self.prog[name]
            def section(eng, prog=prog):
                for fn in prog:
                    fn(eng)
            getattr(block, name)(section)


def build_program():
    _STOPPED[0] = False
    MM_TAGS.clear()
    PHASE[0] = "init"
    nc = bass.Bass("TRN2", target_bir_lowering=False, debug=False)

    x_own = nc.dram_tensor("x_own", [T_CORE, D], F32, kind="ExternalInput").ap()
    x_pre = nc.dram_tensor("x_pre", [T_CORE, D], F32, kind="ExternalInput").ap()
    wqkv = nc.dram_tensor("wqkv", [D, 3 * D], F32, kind="ExternalInput").ap()
    wo = nc.dram_tensor("wo", [D, D], F32, kind="ExternalInput").ap()
    winT = nc.dram_tensor("winT", [D, 4096], F32, kind="ExternalInput").ap()
    w1T = nc.dram_tensor("w1T", [D, 4096], F32, kind="ExternalInput").ap()
    woutT = nc.dram_tensor("woutT", [2048, D], F32, kind="ExternalInput").ap()
    w2T = nc.dram_tensor("w2T", [8 * 2048, D], F32, kind="ExternalInput").ap()
    wgweT = nc.dram_tensor("wgweT", [D, 16], F32, kind="ExternalInput").ap()
    bias_rep = nc.dram_tensor("bias_rep", [128, 16], F32, kind="ExternalInput").ap()
    maskc = nc.dram_tensor("maskc", [128, 128], F32, kind="ExternalInput").ap()
    onehot = nc.dram_tensor("onehot", [8, 8 * 128], F32, kind="ExternalInput").ap()
    ident = nc.dram_tensor("ident", [128, 128], F32, kind="ExternalInput").ap()
    out = nc.dram_tensor("out", [T_CORE, D], F32, kind="ExternalOutput").ap()

    wqkv_bf = nc.dram_tensor("wqkv_bf", [D, 3 * D], BF16, kind="Internal").ap()
    wo_bf = nc.dram_tensor("wo_bf", [D, D], BF16, kind="Internal").ap()
    win_bf = nc.dram_tensor("win_bf", [D, 4096], BF16, kind="Internal").ap()
    w1_bf = nc.dram_tensor("w1_bf", [D, 4096], BF16, kind="Internal").ap()
    wout_bf = nc.dram_tensor("wout_bf", [2048, D], BF16, kind="Internal").ap()
    w2_bf = nc.dram_tensor("w2_bf", [8 * 2048, D], BF16, kind="Internal").ap()

    taps = {}

    with ExitStack() as ctx:
        def sb(name, shape, dt=BF16):
            return ctx.enter_context(nc.sbuf_tensor(name, shape, dt))

        wA = sb("wA", [128, 16384])
        wB = sb("wB", [128, 16384])
        wC = sb("wC", [128, 16384])
        st_sb = sb("st_sb", [128, 1024], F32)
        st_bf = sb("st_bf", [128, 1024])
        mask_sb = sb("mask_sb", [128, 128])
        oneh_sb = sb("oneh_sb", [8, 8 * 128])
        identb = sb("identb", [128, 128])
        wgwe_sb = sb("wgwe_sb", [128, 8 * 16])
        biasr_sb = sb("biasr_sb", [128, 16], F32)
        smalls = sb("smalls", [128, 160], F32)
        wseb = sb("wseb", [128, 32])
        zrow = sb("zrow", [1, 512])
        wse_fm = sb("wse_fm", [8, 512])
        xln = sb("xln", [128, 4 * 1024], F32)
        xn_tm = sb("xn_tm", [128, 2 * 1024])
        xnfm = sb("xnfm", [128, 4096])
        big3 = sb("big3", [128, 12288])
        qfm = big3[:, 0:4096]
        yfm = big3[:, 4096:8192]
        s_tm = big3[:, 8192:12288]
        hbuf = sb("hbuf", [128, 8192])
        ktm = hbuf[:, 0:4096]
        khtm = hbuf[:, 4096:8192]
        vshr = sb("vshr", [128, 4096])
        gsum = hbuf[:, 0:4096]
        vtm = vshr[:, 0:4096]
        shr_tm = vshr[:, 0:4096]
        bc = sb("bc", [128, 8 * 512])
        he = sb("he", [128, 2 * 512])
        scr = sb("scr", [128, 7168])
        khfm = xn_tm[:, 0:1024]
        scoT = scr[:, 0:2048]
        xsq_f = scr[:, 0:2048].bitcast(F32)
        swtmp = [scr[:, 0:1024].bitcast(F32), scr[:, 1024:2048].bitcast(F32)]
        gtmp = [scr[:, 2048:4096].bitcast(F32), scr[:, 4096:6144].bitcast(F32)]
        rsc = scr[:, 5120:7168].bitcast(F32)

        r_lg = rsc[:, 0:16]
        r_d = rsc[:, 16:17]
        r_ad = rsc[:, 17:18]
        r_en = rsc[:, 18:19]
        r_gp = rsc[:, 20:21]
        r_gs = rsc[:, 21:22]
        r_mb0 = rsc[:, 22:23]
        r_mb1 = rsc[:, 23:24]
        r_elm = rsc[:, 24:32]
        r_z = rsc[:, 32:40]
        r_p = rsc[:, 40:48]
        r_eq = rsc[:, 48:56]
        r_pm = rsc[:, 56:64]
        r_keep = rsc[:, 64:72]
        r_m = rsc[:, 72:73]
        r_nm = rsc[:, 73:74]
        r_ss = rsc[:, 74:75]
        r_rs = rsc[:, 75:76]
        r_m1 = rsc[:, 76:77]
        r_m2 = rsc[:, 77:78]

        cpool = sb("cpool", [128, 16], F32)
        sm_bn = [smalls[:, 96 + 12 * c: 108 + 12 * c] for c in range(4)]
        sm_mv = smalls[:, 144:152]
        sm_rs = smalls[:, 152:156]
        sm_lt = smalls[:, 156:160]
        sm_k16 = smalls[:, 16:32]
        sm_kq16 = smalls[:, 32:48]
        sm_kmu = smalls[:, 48:64]
        sm_kvar = smalls[:, 64:80]
        sm_krs = smalls[:, 80:96]

        pa = ctx.enter_context(nc.psum_tensor("pa", [128, 2048], F32))
        pb = ctx.enter_context(nc.psum_tensor("pb", [128, 2048], F32))
        pa_bf = pa[:].bitcast(BF16)
        pb_bf = pb[:].bitcast(BF16)

        semnames = ["sLC", "sCV0", "sCV1", "sCW0", "sCW1", "sWA", "sWB", "sWC",
                    "sLX0", "sLX1", "sLX2", "sLX3", "sST0", "sST1", "sTAP", "sPE", "sV", "sA", "sG"]
        sems = {n: ctx.enter_context(nc.semaphore(n)) for n in semnames}
        block = ctx.enter_context(nc.Block())
        R = Rec(sems)

        def vbar():
            R.wait("vector", "sV")

        def abar():
            R.wait("scalar", "sA")

        def gbar():
            R.wait("gpsimd", "sG")

        _body_guard = True
        CONSTS = [EPS, BIG, -BIG, 1.0, -1.0, 0.0, 1.0 / D, 1.0 / DK, -0.5]
        for i, v in enumerate(CONSTS):
            nc.const_aps.register(F32, v, cpool[:, i:i + 1])

        def tap(name, ap, shape, dt, after_sem):
            if name not in TAPS:
                return
            t_ = nc.dram_tensor("tap_" + name, shape, dt, kind="ExternalOutput").ap()
            taps[name] = t_
            R.wait("gpsimd", after_sem)
            R.inc("gpsimd", lambda e, t_=t_, ap=ap: e.dma_start(out=t_, in_=ap), "sTAP", 16)

        def mm(psum_ap, lhsT, rhs, start, stop, mark=False):
            MM_TAGS.append(PHASE[0])
            fn = lambda e, psum_ap=psum_ap, lhsT=lhsT, rhs=rhs, start=start, stop=stop: \
                e.matmul(psum_ap, lhsT, rhs, start=start, stop=stop)
            if mark:
                return R.inc("tensor", fn, "sPE")
            R.op("tensor", fn)

        R.inc("gpsimd", lambda e: e.dma_start(out=mask_sb[:], in_=maskc), "sLC", 16)
        R.inc("gpsimd", lambda e: e.dma_start(out=oneh_sb[:], in_=onehot), "sLC", 16)
        R.inc("gpsimd", lambda e: e.dma_start(out=identb[:], in_=ident), "sLC", 16)
        R.inc("gpsimd", lambda e: e.dma_start(
            out=wgwe_sb[:].rearrange("p (c n) -> p c n", c=8),
            in_=wgweT.rearrange("(c p) n -> p c n", p=128)), "sLC", 16)
        R.inc("gpsimd", lambda e: e.dma_start(out=biasr_sb[:], in_=bias_rep), "sLC", 16)

        conv_jobs = []
        def conv(src, dst, rows, cols):
            max_rows = max(128, (6144 * 128 // cols) // 128 * 128)
            r0 = 0
            while r0 < rows:
                r1 = min(rows, r0 + max_rows)
                nchunk = (r1 - r0) // 128
                conv_jobs.append((
                    src[r0:r1, :].rearrange("(c p) n -> p c n", p=128),
                    dst[r0:r1, :].rearrange("(c p) n -> p c n", p=128),
                    nchunk, cols))
                r0 = r1
            return len(conv_jobs) - 1

        J_KV = conv(wqkv, wqkv_bf, 1024, 3072)
        J_WO = conv(wo, wo_bf, 1024, 1024)
        J_WIN = conv(winT, win_bf, 1024, 4096)
        J_WOUT = conv(woutT, wout_bf, 2048, 1024)
        J_W1 = conv(w1T, w1_bf, 1024, 4096)
        J_W2 = []
        for e_ in range(8):
            J_W2.append(conv(w2T[e_ * 2048:(e_ + 1) * 2048, :],
                             w2_bf[e_ * 2048:(e_ + 1) * 2048, :], 2048, 1024))

        cw_done = {}
        for ji, (src, dst, nchunk, cols) in enumerate(conv_jobs):
            half = ji % 2
            scv, scw = ("sCV0", "sCW0") if half == 0 else ("sCV1", "sCW1")
            cb = big3[:, half * 6144:(half + 1) * 6144]
            cb3 = cb.rearrange("p (c n) -> p c n", c=nchunk)[:, :, 0:cols]
            R.wait("gpsimd", scw)
            R.inc("gpsimd", lambda e, cb3=cb3, src=src: e.dma_start(out=cb3, in_=src),
                  scv, 16)
            R.wait("sync", scv)
            cw_done[ji] = (scw, R.inc(
                "sync", lambda e, dst=dst, cb3=cb3: e.dma_start(out=dst, in_=cb3),
                scw, 16))
        def cw_wait(eng, ji):
            a = max((cw_done[j][1] for j in range(ji + 1) if cw_done[j][0] == "sCW0"),
                    default=0)
            b = max((cw_done[j][1] for j in range(ji + 1) if cw_done[j][0] == "sCW1"),
                    default=0)
            R.wait(eng, "sCW0", a)
            R.wait(eng, "sCW1", b)
        CW_ALL0 = R.c["sCW0"]
        CW_ALL1 = R.c["sCW1"]

        _ckpt("conv")
        R.wait("vector", "sLC", 16 * 5)
        R.inc("vector", lambda e: e.memset(st_sb[:], 0.0), "sV")
        R.inc("vector", lambda e: e.memset(st_bf[:], 0.0), "sV")
        for i, v in enumerate(CONSTS):
            R.inc("vector", lambda e, i=i, v=v: e.memset(cpool[:, i:i + 1], v), "sV")
        R.inc("vector", lambda e: e.memset(zrow[:], 0.0), "sV")

        def ln_stats(c, transp_marks):
            xv = xln[:, c * 1024:(c + 1) * 1024]
            slx = f"sLX{c}"
            bn = sm_bn[c]
            R.wait("vector", slx)
            R.inc("vector", lambda e, xv=xv, bn=bn: e.bn_stats(
                out=bn[:, 0:6], in_=xv[:, 0:512]), "sV")
            R.inc("vector", lambda e, xv=xv, bn=bn: e.bn_stats(
                out=bn[:, 6:12], in_=xv[:, 512:1024]), "sV")
            vbar()
            aggr = R.inc("vector", lambda e, bn=bn, c=c: e.bn_aggr(
                out=sm_mv[:, 2 * c:2 * c + 2], in_=bn), "sV")
            R.wait("scalar", "sV", aggr)
            R.inc("scalar", lambda e, c=c: e.activation(
                out=sm_lt[:, c:c + 1], in_=sm_mv[:, 2 * c + 1:2 * c + 2],
                func=AF.Ln, bias=EPS), "sA")
            abar()
            rs_mark = R.inc("scalar", lambda e, c=c: e.activation(
                out=sm_rs[:, c:c + 1], in_=sm_lt[:, c:c + 1],
                func=AF.Exp, scale=-0.5), "sA")
            return aggr, rs_mark

        def ln_apply(c, aggr, rs_mark, transp_marks, cps):
            half = c % 2
            xv = xln[:, c * 1024:(c + 1) * 1024]
            xnv = xn_tm[:, half * 1024:(half + 1) * 1024]
            R.wait("vector", "sA", rs_mark)
            R.wait("vector", "sPE", transp_marks.get(c - 2, 0))
            ap_mark = R.inc("vector", lambda e, xv=xv, xnv=xnv, c=c: e.tensor_scalar(
                out=xnv, in0=xv, scalar1=sm_mv[:, 2 * c:2 * c + 1],
                scalar2=sm_rs[:, c:c + 1],
                op0=ALU.subtract, op1=ALU.mult), "sV")
            XV_LAST[c] = ap_mark
            bank_bf = pa_bf[:, (c % 2) * 1024:((c % 2) + 1) * 1024]
            R.wait("tensor", "sV", ap_mark)
            for j in range(8):
                fn = lambda e, xnv=xnv, j=j, bank_bf=bank_bf: e.transpose(
                    bank_bf[:, j * 128:(j + 1) * 128], xnv[:, j * 128:(j + 1) * 128],
                    identb[:])
                if j == 7:
                    transp_marks[c] = R.inc("tensor", fn, "sPE")
                else:
                    R.op("tensor", fn)
            dest = xnfm[:].rearrange("p (j t) -> p j t", j=8)[:, :, c * 128:(c + 1) * 128]
            R.wait("vector", "sPE", transp_marks[c])
            cp = R.inc("vector", lambda e, dest=dest, bank_bf=bank_bf: e.tensor_copy(
                out=dest, in_=bank_bf.rearrange("p (j t) -> p j t", j=8)), "sV")
            return ap_mark, cp

        def ln_tile(xsrc, base_tok, loads_fn):
            transp_marks = {}
            apply_marks = {}
            stats = {}
            cps = {}
            loads_fn(0, apply_marks)
            stats[0] = ln_stats(0, transp_marks)
            cp = 0
            for c in range(N_TC):
                if c + 1 < N_TC:
                    loads_fn(c + 1, apply_marks)
                    stats[c + 1] = ln_stats(c + 1, transp_marks)
                ap, cp = ln_apply(c, stats[c][0], stats[c][1], transp_marks, cps)
                apply_marks[c] = ap
                cps[c] = cp
            return cp

        def knorm_chunk(c):
            kin = ktm[:, c * 1024:(c + 1) * 1024]
            kout = khtm[:, c * 1024:(c + 1) * 1024]
            k3 = kin.rearrange("p (h d) -> p h d", h=16)
            R.wait("scalar", "sV")
            R.inc("scalar", lambda e, kin=kin: e.activation(
                out=xsq_f, in_=kin, func=AF.Square), "sA")
            vbar()
            R.inc("vector", lambda e, k3=k3: e.tensor_reduce(
                out=sm_k16, in_=k3, axis=AX.X, op=ALU.add), "sV")
            R.wait("vector", "sA")
            R.inc("vector", lambda e: e.tensor_reduce(
                out=sm_kq16, in_=xsq_f.rearrange("p (h d) -> p h d", h=16),
                axis=AX.X, op=ALU.add), "sV")
            vbar()
            R.inc("vector", lambda e: e.tensor_scalar(
                out=sm_kmu, in0=sm_k16, scalar1=1.0 / DK, scalar2=None, op0=ALU.mult), "sV")
            R.inc("vector", lambda e: e.tensor_scalar(
                out=sm_kvar, in0=sm_kq16, scalar1=1.0 / DK, scalar2=None, op0=ALU.mult), "sV")
            vbar()
            R.inc("vector", lambda e: e.tensor_tensor(
                out=sm_kq16, in0=sm_kmu, in1=sm_kmu, op=ALU.mult), "sV")
            vbar()
            R.inc("vector", lambda e: e.tensor_tensor(
                out=sm_kvar, in0=sm_kvar, in1=sm_kq16, op=ALU.subtract), "sV")
            R.wait("scalar", "sV")
            R.inc("scalar", lambda e: e.activation(
                out=sm_kq16, in_=sm_kvar, func=AF.Ln, bias=EPS), "sA")
            abar()
            R.inc("scalar", lambda e: e.activation(
                out=sm_krs, in_=sm_kq16, func=AF.Exp, scale=-0.5), "sA")
            mu_b = sm_kmu.unsqueeze(2).broadcast_to((128, 16, 64))
            rs_b = sm_krs.unsqueeze(2).broadcast_to((128, 16, 64))
            ko3 = kout.rearrange("p (h d) -> p h d", h=16)
            R.wait("vector", "sA")
            R.inc("vector", lambda e, k3=k3, ko3=ko3, mu_b=mu_b: e.tensor_tensor(
                out=ko3, in0=k3, in1=mu_b, op=ALU.subtract), "sV")
            vbar()
            return R.inc("vector", lambda e, ko3=ko3, rs_b=rs_b: e.tensor_tensor(
                out=ko3, in0=ko3, in1=rs_b, op=ALU.mult), "sV")

        def kv_phase(tile_kv_ld, xnfm_mark, KN, prime_fn=None):
            R.wait("tensor", "sWC", tile_kv_ld)
            R.wait("vector", "sA", SI_LAST[0])
            ev = {}
            last = 0
            for c in range(N_TC):
                for n in range(4):
                    g = c * 4 + n
                    slot = g % 2
                    ps = pa[:, 1024 + slot * 512: 1024 + (slot + 1) * 512]
                    R.wait("tensor", "sV", ev.get(g - 2, xnfm_mark))
                    for k in range(8):
                        mm(ps,
                           xnfm[:, k * 512 + c * 128: k * 512 + (c + 1) * 128],
                           wC[:, k * 2048 + n * 512: k * 2048 + (n + 1) * 512],
                           start=(k == 0), stop=(k == 7), mark=(k == 7))
                    dst = (ktm if n < 2 else vtm)[:, c * 1024 + (n % 2) * 512:
                                                  c * 1024 + (n % 2) * 512 + 512]
                    R.wait("vector", "sPE")
                    last = ev[g] = R.inc("vector", lambda e, dst=dst, ps=ps: e.tensor_copy(
                        out=dst, in_=ps), "sV")
                if c > 1:
                    KN[c - 2] = knorm_chunk(c - 2)
            KN[N_TC - 3] = knorm_chunk(N_TC - 3)
            if prime_fn is not None:
                prime_fn()
            KN[N_TC - 2] = knorm_chunk(N_TC - 2)
            KN[N_TC - 1] = knorm_chunk(N_TC - 1)
            return last

        def u_chunk(c, kn_mark, convert=True):
            ups = pb[:, (c % 2) * 1024:((c % 2) + 1) * 1024]
            R.wait("tensor", "sV", max(kn_mark, UFREE[c % 2]))
            mm(ups[:, 0:512], zrow[0:1, 0:128], zrow[:], start=True, stop=False)
            mm(ups[:, 512:1024], zrow[0:1, 0:128], zrow[:], start=True, stop=False)
            for p in range(NPAIR):
                base = c * 1024 + p * 128
                mm(ups[0:64, p * 128: p * 128 + 64],
                   khtm[:, base: base + 64], vtm[:, base: base + 64],
                   start=False, stop=False, mark=(p == NPAIR - 1))
            R.wait("tensor", "sPE")
            for p in range(NPAIR):
                base = c * 1024 + p * 128
                mm(ups[64:128, p * 128 + 64: p * 128 + 128],
                   khtm[:, base + 64: base + 128], vtm[:, base + 64: base + 128],
                   start=False, stop=True, mark=(p == NPAIR - 1))
            R.wait("vector", "sPE")
            UFREE[c % 2] = R.inc("vector", lambda e, ups=ups: e.tensor_tensor(
                out=st_sb[:], in0=st_sb[:], in1=ups, op=ALU.add), "sV")
            if convert:
                vbar()
                R.inc("vector", lambda e: e.tensor_copy(out=st_bf[:], in_=st_sb[:]), "sV")

        def kht_group(c, wait_sV):
            R.wait("tensor", "sV", wait_sV)
            MM_TAGS.extend(["khT"] * 8)
            km = None
            for p in range(NPAIR):
                fn = lambda e, c=c, p=p: e.transpose(
                    pa_bf[:, p * 128:(p + 1) * 128],
                    khtm[:, c * 1024 + p * 128: c * 1024 + (p + 1) * 128],
                    identb[:])
                if p == NPAIR - 1:
                    km = R.inc("tensor", fn, "sPE")
                else:
                    R.op("tensor", fn)
            R.wait("vector", "sPE", km)
            return R.inc("vector", lambda e: e.tensor_copy(
                out=khfm, in_=pa_bf[:, 0:1024]), "sV")

        PHASE[0] = "prefix"
        ST_CV = 0
        UFREE = [0, 0]
        SI_LAST = {0: 0}
        XV_LAST = {0: 0, 1: 0, 2: 0, 3: 0}
        R.wait("tensor", "sLC", 16 * 5)
        kv_ld = 0
        for j4 in range(J_KV + 1):
            cw_wait("sync", j4)
            kv_ld = R.inc("sync", lambda e, j4=j4: e.dma_start(
                out=wC[:, j4 * 4096:(j4 + 1) * 4096].rearrange(
                    "p (c n) -> p c n", c=2),
                in_=wqkv_bf[j4 * 256:(j4 + 1) * 256, 1024:3072].rearrange(
                    "(c p) n -> p c n", p=128)),
                "sWC", 16)
        for pt in range(N_PRE):
            def pre_loads(c, apply_marks):
                xv = xln[:, c * 1024:(c + 1) * 1024]
                R.wait("gpsimd", "sV", XV_LAST[c])
                R.inc("gpsimd", lambda e, xv=xv, pt=pt, c=c: e.dma_start(
                    out=xv, in_=x_pre[(pt * 4 + c) * 128:(pt * 4 + c + 1) * 128, :]),
                    f"sLX{c}", 16)
            xnfm_mark = ln_tile(x_pre, pt * 4, pre_loads)
            KN = {}
            kv_phase(kv_ld, xnfm_mark, KN)
            for c in range(N_TC):
                u_chunk(c, KN[c], convert=(pt == N_PRE - 1 and c == N_TC - 1))
        tap("st_prefix", st_sb[:], [128, 1024], F32, "sV")
        ST_CV = R.c["sV"]
        _ckpt("prefix")

        ld_mark = {}
        cw_wait("sync", J_WO)
        R.inc("sync", lambda e: e.dma_start(
            out=wB[:, 0:8192].rearrange("p (c n) -> p c n", c=8),
            in_=wqkv_bf[:, 0:1024].rearrange("(c p) n -> p c n", p=128)), "sWB", 16)
        ld_mark["wqwo", 0] = R.inc("sync", lambda e: e.dma_start(
            out=wB[:, 8192:16384].rearrange("p (c n) -> p c n", c=8),
            in_=wo_bf.rearrange("(c p) n -> p c n", p=128)), "sWB", 16)
        cw_wait("sync", J_WIN)
        ld_mark["winA", 0] = R.inc("sync", lambda e: e.dma_start(
            out=wA[:].rearrange("p (c n) -> p c n", c=8),
            in_=win_bf[:, 0:2048].rearrange("(c p) n -> p c n", p=128)), "sWA", 16)
        R.wait("sync", "sPE")
        ld_mark["wkv", 0] = R.inc("sync", lambda e: e.dma_start(
            out=wC[:].rearrange("p (c n) -> p c n", c=8),
            in_=wqkv_bf[:, 1024:3072].rearrange("(c p) n -> p c n", p=128)), "sWC", 16)

        NEXT_STATS = {}
        LAST_GP_MUL = {}
        for t in range(N_TILES):
            if t == 0:
                def main_loads(c, apply_marks, t=t):
                    xv = xln[:, c * 1024:(c + 1) * 1024]
                    R.wait("gpsimd", "sV", XV_LAST[c])
                    R.inc("gpsimd", lambda e, xv=xv, t=t, c=c: e.dma_start(
                        out=xv, in_=x_own[(t * 4 + c) * 128:(t * 4 + c + 1) * 128, :]),
                        f"sLX{c}", 16)
                xnfm_mark = ln_tile(x_own, t * 4, main_loads)
            else:
                transp_marks = {}
                cps = {}
                for c in range(N_TC):
                    _, xnfm_mark = ln_apply(c, NEXT_STATS[c][0], NEXT_STATS[c][1],
                                            transp_marks, cps)
                    cps[c] = xnfm_mark
            if t == 0:
                tap("xnfm", xnfm[:], [128, 4096], BF16, "sV")
                _ckpt("ln0")

            PHASE[0] = "q"
            if t == 0:
                R.wait("vector", "sCW0", CW_ALL0)
                R.wait("vector", "sCW1", CW_ALL1)
                R.wait("tensor", "sLC", 16 * 5)
            R.wait("tensor", "sWB", ld_mark["wqwo", t])
            evq = {}
            for m in range(8):
                ps = pa[:, (m % 2) * 512:((m % 2) + 1) * 512]
                R.wait("tensor", "sV", evq.get(m - 2, xnfm_mark))
                for k in range(8):
                    mm(ps,
                       wB[:, k * 1024 + m * 128: k * 1024 + (m + 1) * 128],
                       xnfm[:, k * 512:(k + 1) * 512],
                       start=(k == 0), stop=(k == 7), mark=(k == 7))
                R.wait("vector", "sPE")
                evq[m] = R.inc("vector", lambda e, m=m, ps=ps: e.tensor_copy(
                    out=qfm[:, m * 512:(m + 1) * 512], in_=ps), "sV")

            PHASE[0] = "kv"
            KN = {}
            PRIME = {}

            def _prime():
                PRIME[0] = kht_group(0, KN[0])
            kv_phase(ld_mark["wkv", t], evq[7], KN, _prime)
            kv_done = R.c["sPE"]
            R.wait("sync", "sPE", kv_done)
            ld_mark["winB", t] = R.inc("sync", lambda e: e.dma_start(
                out=wC[:].rearrange("p (c n) -> p c n", c=8),
                in_=win_bf[:, 2048:4096].rearrange("(c p) n -> p c n", p=128)),
                "sWC", 16)

            if t == 0:
                tap("smalls", smalls[:], [128, 160], F32, "sV")
                tap("xv1", xln[:, 1024:2048], [128, 1024], F32, "sV")
                tap("khtm", khtm[:], [128, 4096], BF16, "sV")
                tap("vtm", vtm[:], [128, 4096], BF16, "sV")
                tap("qfm", qfm[:], [128, 4096], BF16, "sV")
                _ckpt("qkv0")

            PHASE[0] = "scan"
            KE = PRIME[0]
            SE = {}
            YE = {-1: 0}
            CV = {-1: ST_CV}
            MG = {}
            for c in range(N_TC):
                R.wait("tensor", "sV", max(YE[c - 1], KE, CV[c - 1]))
                mm(pb[:, 0:512], zrow[0:1, 0:128], zrow[:], start=True, stop=False)
                mm(pb[:, 512:1024], zrow[0:1, 0:128], zrow[:], start=True, stop=False)
                for p in range(NPAIR):
                    mm(pb[:, p * 128:(p + 1) * 128],
                       st_bf[:, p * 128:(p + 1) * 128],
                       qfm[:, p * 512 + c * 128: p * 512 + (c + 1) * 128],
                       start=False, stop=False)
                for h in range(0, H, 2):
                    p = h // 2
                    mm(pa[:, h * 128:(h + 1) * 128],
                       khfm[0:64, p * 128:(p + 1) * 128],
                       qfm[0:64, p * 512 + c * 128: p * 512 + (c + 1) * 128],
                       start=True, stop=True)
                ups = pb[:, 1024:2048]
                mm(ups[:, 0:512], zrow[0:1, 0:128], zrow[:], start=True, stop=False)
                mm(ups[:, 512:1024], zrow[0:1, 0:128], zrow[:], start=True, stop=False)
                for p in range(NPAIR):
                    base = c * 1024 + p * 128
                    mm(ups[0:64, p * 128: p * 128 + 64],
                       khtm[:, base: base + 64], vtm[:, base: base + 64],
                       start=False, stop=False)
                    mm(ups[64:128, p * 128 + 64: p * 128 + 128],
                       khtm[:, base + 64: base + 128], vtm[:, base + 64: base + 128],
                       start=False, stop=(p == NPAIR - 1), mark=(p == NPAIR - 1))
                mu_ = R.c["sPE"]
                for h in range(1, H, 2):
                    p = h // 2
                    mm(pa[:, h * 128:(h + 1) * 128],
                       khfm[64:128, p * 128:(p + 1) * 128],
                       qfm[64:128, p * 512 + c * 128: p * 512 + (c + 1) * 128],
                       start=True, stop=True, mark=(h == H - 1))
                ms = R.c["sPE"]
                R.wait("vector", "sPE", ms)
                mask3 = mask_sb[:].unsqueeze(1).broadcast_to((128, 16, 128))
                SE[c] = R.inc("vector", lambda e, mask3=mask3: e.tensor_tensor(
                    out=scoT.rearrange("p (h t) -> p h t", h=16),
                    in0=pa[:].rearrange("p (h t) -> p h t", h=16),
                    in1=mask3, op=ALU.mult), "sV")
                MG[c] = R.inc("vector", lambda e, ups=ups: e.tensor_tensor(
                    out=st_sb[:], in0=st_sb[:], in1=ups, op=ALU.add), "sV")
                R.wait("vector", "sV", MG[c])
                CV[c] = R.inc("vector", lambda e: e.tensor_copy(
                    out=st_bf[:], in_=st_sb[:]), "sV")
                if c + 1 < N_TC:
                    KE = kht_group(c + 1, max(SE[c], KN[c + 1]))
                R.wait("tensor", "sV", SE[c])
                for h in range(H):
                    p, u = h // 2, h % 2
                    mm(pb[u * 64:(u + 1) * 64, p * 128:(p + 1) * 128],
                       vtm[:, c * 1024 + p * 128 + u * 64: c * 1024 + p * 128 + u * 64 + 64],
                       scoT[:, h * 128:(h + 1) * 128],
                       start=False, stop=True, mark=(h == H - 1))
                mi = R.c["sPE"]
                ydest = yfm[:].rearrange("p (j t) -> p j t", j=8)[:, :, c * 128:(c + 1) * 128]
                R.wait("vector", "sPE", mi)
                YE[c] = R.inc("vector", lambda e, ydest=ydest: e.tensor_copy(
                    out=ydest, in_=pb[:, 0:1024].rearrange("p (j t) -> p j t", j=8)), "sV")
            ST_CV = CV[N_TC - 1]
            if t == 0:
                tap("yfm", yfm[:], [128, 4096], BF16, "sV")
                tap("st_tile0", st_sb[:], [128, 1024], F32, "sV")
                _ckpt("scan0")

            PHASE[0] = "wo"
            y_mark = R.c["sV"]
            evs = {}
            for c in range(N_TC):
                ps = pa[:, (c % 2) * 1024:((c % 2) + 1) * 1024]
                R.wait("tensor", "sV", evs.get(c - 2, y_mark))
                for p in range(8):
                    for n in range(2):
                        mm(ps[:, n * 512:(n + 1) * 512],
                           yfm[:, p * 512 + c * 128: p * 512 + (c + 1) * 128],
                           wB[:, 8192 + p * 1024 + n * 512: 8192 + p * 1024 + (n + 1) * 512],
                           start=(p == 0), stop=(p == 7), mark=(p == 7 and n == 1))
                R.wait("vector", "sPE")
                evs[c] = R.inc("vector", lambda e, c=c, ps=ps: e.tensor_copy(
                    out=s_tm[:, c * 1024:(c + 1) * 1024], in_=ps), "sV")
            wo_done = R.c["sPE"]
            if t == 0:
                tap("s_tm", s_tm[:], [128, 4096], BF16, "sV")
                _ckpt("wo0")

            PHASE[0] = "router"
            lgB = rsc[:, 0:64]
            lg3 = lgB.rearrange("p (c n) -> p c n", c=4)
            dB = rsc[:, 64:68]
            adB = rsc[:, 68:72]
            enB = rsc[:, 72:76]
            gpB = rsc[:, 76:80]
            gsB = rsc[:, 80:84]
            mb0B = rsc[:, 84:88]
            mb1B = rsc[:, 88:92]
            elm3 = rsc[:, 96:128].rearrange("p (c n) -> p c n", c=4)
            z3 = rsc[:, 128:160].rearrange("p (c n) -> p c n", c=4)
            p3 = rsc[:, 160:192].rearrange("p (c n) -> p c n", c=4)
            eq3 = rsc[:, 192:224].rearrange("p (c n) -> p c n", c=4)
            pm3 = rsc[:, 224:256].rearrange("p (c n) -> p c n", c=4)
            keep3 = rsc[:, 256:288].rearrange("p (c n) -> p c n", c=4)
            tmp3 = rsc[:, 288:320].rearrange("p (c n) -> p c n", c=4)
            mB = rsc[:, 320:324]
            ssB = rsc[:, 324:328]
            rsB = rsc[:, 328:332]
            m1B = rsc[:, 332:336]
            m2B = rsc[:, 336:340]

            def b3(x, w=8):
                return x.unsqueeze(2).broadcast_to((128, 4, w))

            s_mark = R.c["sV"]
            evr = {}
            for c in range(N_TC):
                ps = pb[:, (c % 2) * 512:((c % 2) + 1) * 512]
                R.wait("tensor", "sV", evr.get(c - 2, s_mark))
                for k in range(8):
                    mm(ps[:, 0:16],
                       xnfm[:, k * 512 + c * 128: k * 512 + (c + 1) * 128],
                       wgwe_sb[:, k * 16:(k + 1) * 16],
                       start=(k == 0), stop=(k == 7), mark=(k == 7))
                R.wait("vector", "sPE")
                evr[c] = R.inc("vector", lambda e, ps=ps, c=c: e.tensor_copy(
                    out=lgB[:, c * 16:(c + 1) * 16], in_=ps[:, 0:16]), "sV")
            vbar()
            bias3 = biasr_sb[:].unsqueeze(1).broadcast_to((128, 4, 16))
            R.inc("vector", lambda e: e.tensor_tensor(
                out=lg3, in0=lg3, in1=bias3, op=ALU.add), "sV")
            vbar()
            R.inc("vector", lambda e: e.tensor_tensor(
                out=dB, in0=lg3[:, :, 0:1], in1=lg3[:, :, 1:2], op=ALU.subtract), "sV")
            R.inc("vector", lambda e: e.tensor_tensor(
                out=gsB, in0=lg3[:, :, 1:2], in1=lg3[:, :, 0:1], op=ALU.is_gt), "sV")
            R.wait("scalar", "sV")
            R.inc("scalar", lambda e: e.activation(
                out=adB, in_=dB, func=AF.Abs), "sA")
            abar()
            R.inc("scalar", lambda e: e.activation(
                out=enB, in_=adB, func=AF.Exp, scale=-1.0), "sA")
            R.wait("vector", "sA")
            R.inc("vector", lambda e: e.tensor_scalar(
                out=enB, in0=enB, scalar1=1.0, scalar2=None, op0=ALU.add), "sV")
            vbar()
            R.inc("vector", lambda e: e.reciprocal(out=gpB, in_=enB), "sV")
            R.inc("vector", lambda e: e.tensor_scalar(
                out=mb1B, in0=gsB, scalar1=BIG, scalar2=BIG,
                op0=ALU.mult, op1=ALU.subtract), "sV")
            R.inc("vector", lambda e: e.tensor_scalar(
                out=mb0B, in0=gsB, scalar1=-BIG, scalar2=None, op0=ALU.mult), "sV")
            vbar()
            R.inc("vector", lambda e: e.tensor_tensor(
                out=elm3[:, :, 0:4], in0=lg3[:, :, 2:6], in1=b3(mb0B, 4),
                op=ALU.add), "sV")
            R.inc("vector", lambda e: e.tensor_tensor(
                out=elm3[:, :, 4:8], in0=lg3[:, :, 6:10], in1=b3(mb1B, 4),
                op=ALU.add), "sV")
            vbar()
            R.inc("vector", lambda e: e.tensor_reduce(
                out=mB, in_=elm3, axis=AX.X, op=ALU.max), "sV")
            vbar()
            R.inc("vector", lambda e: e.tensor_tensor(
                out=elm3, in0=elm3, in1=b3(mB), op=ALU.subtract), "sV")
            R.wait("scalar", "sV")
            R.inc("scalar", lambda e: e.activation(
                out=z3, in_=elm3, func=AF.Exp), "sA")
            R.wait("vector", "sA")
            R.inc("vector", lambda e: e.tensor_reduce(
                out=ssB, in_=z3, axis=AX.X, op=ALU.add), "sV")
            vbar()
            R.inc("vector", lambda e: e.reciprocal(out=rsB, in_=ssB), "sV")
            vbar()
            R.inc("vector", lambda e: e.tensor_tensor(
                out=tmp3, in0=z3, in1=b3(rsB), op=ALU.mult), "sV")
            vbar()
            R.inc("vector", lambda e: e.tensor_tensor(
                out=p3, in0=tmp3, in1=b3(gpB), op=ALU.mult), "sV")
            vbar()
            R.inc("vector", lambda e: e.tensor_reduce(
                out=m1B, in_=p3, axis=AX.X, op=ALU.max), "sV")
            vbar()
            R.inc("vector", lambda e: e.tensor_tensor(
                out=eq3, in0=p3, in1=b3(m1B), op=ALU.is_ge), "sV")
            vbar()
            R.inc("vector", lambda e: e.scalar_tensor_tensor(
                out=pm3, in0=eq3, scalar=-BIG, in1=p3,
                op0=ALU.mult, op1=ALU.add), "sV")
            vbar()
            R.inc("vector", lambda e: e.tensor_reduce(
                out=m2B, in_=pm3, axis=AX.X, op=ALU.max), "sV")
            vbar()
            R.inc("vector", lambda e: e.tensor_tensor(
                out=keep3, in0=p3, in1=b3(m2B), op=ALU.is_ge), "sV")
            vbar()
            R.inc("vector", lambda e: e.tensor_tensor(
                out=wseb[:].rearrange("p (c n) -> p c n", c=4),
                in0=p3, in1=keep3, op=ALU.mult), "sV")
            for c in range(N_TC):
                R.wait("tensor", "sV")
                MM_TAGS.append("wseT")
                R.inc("tensor", lambda e, c=c: e.transpose(
                    pb_bf[0:8, 2048:2048 + 128], wseb[:, c * 8:(c + 1) * 8],
                    identb[:]), "sPE")
                R.wait("vector", "sPE")
                R.inc("vector", lambda e, c=c: e.tensor_copy(
                    out=wse_fm[:, c * 128:(c + 1) * 128],
                    in_=pb_bf[0:8, 2048:2048 + 128]), "sV")
            evb = {}
            wse_mark = R.c["sV"]
            for e_ in range(8):
                ps = pb[:, (e_ % 2) * 512:((e_ % 2) + 1) * 512]
                R.wait("tensor", "sV", evb.get(e_ - 2, wse_mark))
                mm(ps, oneh_sb[:, e_ * 128:(e_ + 1) * 128], wse_fm[:],
                   start=True, stop=True, mark=True)
                R.wait("vector", "sPE")
                evb[e_] = R.inc("vector", lambda e, e_=e_, ps=ps: e.tensor_copy(
                    out=bc[:, e_ * 512:(e_ + 1) * 512], in_=ps), "sV")
            router_done = evb[7]
            if t == 0:
                tap("wse_fm", wse_fm[:], [8, 512], BF16, "sV")
                tap("bc", bc[:], [128, 4096], BF16, "sV")
                _ckpt("router0")

            R.wait("sync", "sPE", wo_done)
            cw_wait("sync", J_WOUT)
            ld_mark["wout", t] = R.inc("sync", lambda e: e.dma_start(
                out=wB[:].rearrange("p (c n) -> p c n", c=16),
                in_=wout_bf.rearrange("(c p) n -> p c n", p=128)), "sWB", 16)

            def swiglu_phase(srcA_is_w1, in_ready_sV):
                silu_m = {}
                mult_m = {}
                for halfp in range(2):
                    reg = wA if halfp == 0 else wC
                    key = (("w1A" if halfp == 0 else "w1B") if srcA_is_w1
                           else ("winA" if halfp == 0 else "winB"))
                    R.wait("tensor", "sWA" if halfp == 0 else "sWC", ld_mark[key, t])
                    for pr in range(8):
                        g = halfp * 8 + pr
                        j = g
                        slot = pr % 4
                        pban = pa if slot < 2 else pb
                        pa_a = pban[:, (slot % 2) * 1024: (slot % 2) * 1024 + 512]
                        pa_b = pban[:, (slot % 2) * 1024 + 512: (slot % 2) * 1024 + 1024]
                        R.wait("tensor", "sV", mult_m.get(g - 4, in_ready_sV))
                        R.wait("tensor", "sA", silu_m.get(g - 4, 0))
                        for k in range(8):
                            mm(pa_a,
                               reg[:, k * 2048 + pr * 256: k * 2048 + pr * 256 + 128],
                               xnfm[:, k * 512:(k + 1) * 512],
                               start=(k == 0), stop=(k == 7))
                        for k in range(8):
                            mm(pa_b,
                               reg[:, k * 2048 + pr * 256 + 128: k * 2048 + pr * 256 + 256],
                               xnfm[:, k * 512:(k + 1) * 512],
                               start=(k == 0), stop=(k == 7), mark=(k == 7))
                        sw = swtmp[slot % 2]
                        R.wait("scalar", "sPE")
                        R.wait("scalar", "sV", mult_m.get(g - 2, 0))
                        silu_m[g] = R.inc("scalar", lambda e, sw=sw, pa_a=pa_a: e.activation(
                            out=sw, in_=pa_a, func=AF.Silu), "sA")
                        R.wait("vector", "sA", silu_m[g])
                        mult_m[g] = R.inc(
                            "vector", lambda e, j=j, sw=sw, pa_b=pa_b: e.tensor_tensor(
                                out=hbuf[:, j * 512:(j + 1) * 512], in0=sw, in1=pa_b,
                                op=ALU.mult), "sV")
                    R.wait("sync", "sPE")
                    regsem = "sWA" if halfp == 0 else "sWC"
                    if not srcA_is_w1:
                        cw_wait("sync", J_W1)
                        ld_mark[("w1A" if halfp == 0 else "w1B"), t] = R.inc(
                            "sync", lambda e, reg=reg, halfp=halfp: e.dma_start(
                                out=reg[:].rearrange("p (c n) -> p c n", c=8),
                                in_=w1_bf[:, halfp * 2048:(halfp + 1) * 2048].rearrange(
                                    "(c p) n -> p c n", p=128)), regsem, 16)
                    else:
                        e_ = halfp
                        cw_wait("sync", J_W2[e_])
                        ld_mark["w2", t, e_] = R.inc(
                            "sync", lambda e, reg=reg, e_=e_: e.dma_start(
                                out=reg[:].rearrange("p (c n) -> p c n", c=16),
                                in_=w2_bf[e_ * 2048:(e_ + 1) * 2048, :].rearrange(
                                    "(c p) n -> p c n", p=128)), regsem, 16)
                return mult_m[15], silu_m[15]

            PHASE[0] = "w_in"
            hs_sV, hs_sA = swiglu_phase(False, router_done)
            if t == 0:
                tap("hs", hbuf[:], [128, 8192], BF16, "sV")
                _ckpt("win0")

            PHASE[0] = "w_out"
            R.wait("tensor", "sWB", ld_mark["wout", t])
            evw = {}
            for c in range(N_TC):
                ps = pb[:, (c % 2) * 1024:((c % 2) + 1) * 1024]
                R.wait("tensor", "sV", evw.get(c - 2, hs_sV))
                R.wait("tensor", "sA", hs_sA if c < 2 else 0)
                for j in range(16):
                    for n in range(2):
                        mm(ps[:, n * 512:(n + 1) * 512],
                           hbuf[:, j * 512 + c * 128: j * 512 + (c + 1) * 128],
                           wB[:, j * 1024 + n * 512: j * 1024 + (n + 1) * 512],
                           start=(j == 0), stop=(j == 15), mark=(j == 15 and n == 1))
                R.wait("vector", "sPE")
                evw[c] = R.inc("vector", lambda e, c=c, ps=ps: e.tensor_copy(
                    out=shr_tm[:, c * 1024:(c + 1) * 1024], in_=ps), "sV")
            shr_last = evw[N_TC - 1]
            if t == 0:
                tap("shr_tm", shr_tm[:], [128, 4096], BF16, "sV")
                _ckpt("wout0")

            PHASE[0] = "w1"
            h_sV, h_sA = swiglu_phase(True, shr_last)
            if t == 0:
                tap("h", hbuf[:], [128, 8192], BF16, "sV")
                _ckpt("w1_0")

            PHASE[0] = "routed"
            pregion = [pa[:, 0:1024], pa[:, 1024:2048], pb[:, 0:1024], pb[:, 1024:2048]]
            R.wait("vector", "sV")
            he_marks = []
            for e_ in range(8):
                reg = wA if e_ % 2 == 0 else wC
                R.wait("tensor", "sWA" if e_ % 2 == 0 else "sWC", ld_mark["w2", t, e_])
                for j in range(16):
                    idx = e_ * 16 + j
                    hv = he[:, (idx % 2) * 512:((idx % 2) + 1) * 512]
                    if idx >= 2:
                        R.wait("vector", "sPE", he_marks[idx - 2])
                    else:
                        R.wait("vector", "sPE")
                    R.inc("vector", lambda e, hv=hv, j=j, e_=e_: e.tensor_tensor(
                        out=hv, in0=hbuf[:, j * 512:(j + 1) * 512],
                        in1=bc[:, e_ * 512:(e_ + 1) * 512], op=ALU.mult), "sV")
                    R.wait("tensor", "sV")
                    lastmark = None
                    for c in range(N_TC):
                        for n in range(2):
                            mm(pregion[c][:, n * 512:(n + 1) * 512],
                               hv[:, c * 128:(c + 1) * 128],
                               reg[:, j * 1024 + n * 512: j * 1024 + (n + 1) * 512],
                               start=(e_ == 0 and j == 0),
                               stop=(e_ == 7 and j == 15),
                               mark=(c == N_TC - 1 and n == 1))
                    he_marks.append(R.c["sPE"])
                if e_ + 2 <= 7:
                    R.wait("sync", "sPE")
                    cw_wait("sync", J_W2[e_ + 2])
                    ld_mark["w2", t, e_ + 2] = R.inc(
                        "sync", lambda e, reg=reg, e_=e_: e.dma_start(
                            out=reg[:].rearrange("p (c n) -> p c n", c=16),
                            in_=w2_bf[(e_ + 2) * 2048:(e_ + 3) * 2048, :].rearrange(
                                "(c p) n -> p c n", p=128)),
                        "sWA" if e_ % 2 == 0 else "sWC", 16)
                elif e_ == 6 and t + 1 < N_TILES:
                    R.wait("sync", "sPE")
                    R.inc("sync", lambda e: e.dma_start(
                        out=wB[:, 0:8192].rearrange("p (c n) -> p c n", c=8),
                        in_=wqkv_bf[:, 0:1024].rearrange("(c p) n -> p c n", p=128)),
                        "sWB", 16)
                    ld_mark["wqwo", t + 1] = R.inc("sync", lambda e: e.dma_start(
                        out=wB[:, 8192:16384].rearrange("p (c n) -> p c n", c=8),
                        in_=wo_bf.rearrange("(c p) n -> p c n", p=128)), "sWB", 16)
                    ld_mark["winA", t + 1] = R.inc("sync", lambda e: e.dma_start(
                        out=wA[:].rearrange("p (c n) -> p c n", c=8),
                        in_=win_bf[:, 0:2048].rearrange("(c p) n -> p c n", p=128)),
                        "sWA", 16)
                elif e_ == 7 and t + 1 < N_TILES:
                    R.wait("sync", "sPE")
                    ld_mark["wkv", t + 1] = R.inc("sync", lambda e: e.dma_start(
                        out=wC[:].rearrange("p (c n) -> p c n", c=8),
                        in_=wqkv_bf[:, 1024:3072].rearrange("(c p) n -> p c n", p=128)),
                        "sWC", 16)

            if t == 0:
                _ckpt("routed0")
            GA = {}
            SI = {}
            R.wait("vector", "sA", SI_LAST[0])
            for c in range(N_TC):
                R.wait("vector", "sPE")
                GA[c] = R.inc("vector", lambda e, c=c: e.tensor_tensor(
                    out=gsum[:, c * 1024:(c + 1) * 1024], in0=pregion[c],
                    in1=shr_tm[:, c * 1024:(c + 1) * 1024], op=ALU.add), "sV")
            for c in range(N_TC):
                R.wait("scalar", "sV", GA[c])
                SI[c] = R.inc("scalar", lambda e, c=c: e.activation(
                    out=gsum[:, c * 1024:(c + 1) * 1024],
                    in_=gsum[:, c * 1024:(c + 1) * 1024], func=AF.Silu), "sA")
            SI_LAST[0] = SI[N_TC - 1]
            for c in range(N_TC):
                half = c % 2
                xv = xln[:, c * 1024:(c + 1) * 1024]
                gv = gtmp[half]
                R.wait("gpsimd", "sA", SI[c])
                R.wait("gpsimd", "sST0" if half == 0 else "sST1")
                LAST_GP_MUL[c] = R.inc("gpsimd", lambda e, gv=gv, c=c: e.tensor_tensor(
                    out=gv, in0=gsum[:, c * 1024:(c + 1) * 1024],
                    in1=s_tm[:, c * 1024:(c + 1) * 1024],
                    op=ALU.mult), "sG")
                R.wait("gpsimd", "sG", LAST_GP_MUL[c])
                XV_LAST[c] = R.inc("gpsimd", lambda e, gv=gv, xv=xv: e.tensor_tensor(
                    out=gv, in0=gv, in1=xv, op=ALU.add), "sG")
                R.wait("gpsimd", "sG", XV_LAST[c])
                R.inc("gpsimd", lambda e, gv=gv, t=t, c=c: e.dma_start(
                    out=out[(t * 4 + c) * 128:(t * 4 + c + 1) * 128, :], in_=gv),
                    "sST0" if half == 0 else "sST1", 16)
                if t + 1 < N_TILES:
                    R.inc("gpsimd", lambda e, xv=xv, t=t, c=c: e.dma_start(
                        out=xv, in_=x_own[((t + 1) * 4 + c) * 128:
                                          ((t + 1) * 4 + c + 1) * 128, :]),
                        f"sLX{c}", 16)
            if t + 1 < N_TILES:
                for c in range(N_TC):
                    NEXT_STATS[c] = ln_stats(c, None)

        R.wait("tensor", "sST0")
        R.wait("tensor", "sST1")
        R.wait("tensor", "sTAP")
        R.wait("tensor", "sG")
        R.wait("tensor", "sV")
        R.wait("tensor", "sA")
        R.emit(block)

    return nc, taps


def kernel(**inputs):
    x = np.ascontiguousarray(inputs["L0_attn_x"], dtype=np.float32)
    W_qkv = np.ascontiguousarray(inputs["L0_attn_W_qkv"], dtype=np.float32)
    W_o = np.ascontiguousarray(inputs["L0_attn_W_o"], dtype=np.float32)
    w_in = np.asarray(inputs["L0_ffn_w_in"], dtype=np.float32)
    w_out = np.asarray(inputs["L0_ffn_w_out"], dtype=np.float32)
    w1 = np.asarray(inputs["L0_ffn_w1"], dtype=np.float32)
    w2 = np.asarray(inputs["L0_ffn_w2"], dtype=np.float32)
    wg = np.asarray(inputs["L0_ffn_wg"], dtype=np.float32)
    we = np.asarray(inputs["L0_ffn_we"], dtype=np.float32)
    gb = np.asarray(inputs["L0_ffn_gb"], dtype=np.float32)
    eb = np.asarray(inputs["L0_ffn_eb"], dtype=np.float32)

    def paired(wT):
        a = wT[:, :2048].reshape(1024, 16, 128)
        b = wT[:, 2048:].reshape(1024, 16, 128)
        return np.ascontiguousarray(np.stack([a, b], axis=2).reshape(1024, 4096))

    winT = paired(np.ascontiguousarray(w_in.T))
    w1T = paired(np.ascontiguousarray(w1.T))
    woutT = np.ascontiguousarray(w_out.T)
    w2T = np.ascontiguousarray(w2.transpose(0, 2, 1).reshape(8 * 2048, 1024))
    wgweT = np.zeros((1024, 16), np.float32)
    wgweT[:, 0:2] = wg.T
    wgweT[:, 2:10] = we.T
    bias_rep = np.zeros((128, 16), np.float32)
    bias_rep[:, 0:2] = gb[None, :]
    bias_rep[:, 2:10] = eb[None, :]
    maskc = np.triu(np.ones((128, 128), np.float32))
    onehot = np.zeros((8, 8 * 128), np.float32)
    for e in range(8):
        onehot[e, e * 128:(e + 1) * 128] = 1.0
    identm = np.eye(128, dtype=np.float32)
    zeros_pre = np.zeros((T_CORE, D), np.float32)

    nc, taps = build_program()
    shared = {
        "wqkv": W_qkv, "wo": W_o, "winT": winT, "w1T": w1T, "woutT": woutT,
        "w2T": w2T, "wgweT": wgweT, "bias_rep": bias_rep, "maskc": maskc,
        "onehot": onehot, "ident": identm,
    }
    in_maps = []
    for c in range(8):
        b, half = c // 2, c % 2
        m = dict(shared)
        m["x_own"] = np.ascontiguousarray(x[b, half * T_CORE:(half + 1) * T_CORE, :])
        m["x_pre"] = (zeros_pre if half == 0 else
                      np.ascontiguousarray(x[b, 0:T_CORE, :]))
        in_maps.append(m)

    res = run_bass_kernel_spmd(nc, in_maps, core_ids=list(range(8)))
    outp = np.empty((4, 8192, 1024), np.float32)
    for c in range(8):
        b, half = c // 2, c % 2
        outp[b, half * T_CORE:(half + 1) * T_CORE, :] = res.results[c]["out"]
    kernel.tap_results = res.results
    return outp


# version 25
# speedup vs baseline: 1.0506x; 1.0506x over previous
import numpy as np
import concourse.bass as bass
import concourse.bass_utils as _bu
import concourse.mybir as mybir
from concourse.bass_utils import run_bass_kernel_spmd
from contextlib import ExitStack

if not getattr(_bu, "_ldw_opt_patched", False):
    _bu._ldw_opt_patched = True
    _orig_run_command = _bu.run_command

    def _run_command_ldw(argv, **kw):
        argv = [a.replace("--enable-ldw-opt=false", "--enable-ldw-opt=true")
                if isinstance(a, str) else a for a in argv]
        return _orig_run_command(argv, **kw)

    _bu.run_command = _run_command_ldw

F32 = mybir.dt.float32
BF16 = mybir.dt.bfloat16
AF = mybir.ActivationFunctionType
ALU = mybir.AluOpType
AX = mybir.AxisListType

T_CORE = 4096
D = 1024
H = 16
DK = 64
NPAIR = 8
N_TILES = 8
N_PRE = 8
N_TC = 4
EPS = 1e-5
BIG = 1.0e30

TAPS = []
MM_TAGS = []
PHASE = ["?"]
STOP_AFTER = None
SKIP_INTER = False
SKIP_SCORES = False
ENGINES = ("sync", "gpsimd", "vector", "scalar", "tensor")


_STOPPED = [False]


def _ckpt(name):
    if STOP_AFTER == name:
        _STOPPED[0] = True


class Rec:
    def __init__(self, sems):
        self.prog = {e: [] for e in ENGINES}
        self.sems = sems
        self.c = {n: 0 for n in sems}

    def op(self, eng, fn):
        if _STOPPED[0]:
            return
        self.prog[eng].append(fn)

    def inc(self, eng, fn, sem, amt=1):
        if _STOPPED[0]:
            return self.c[sem]
        h = self.sems[sem]
        self.prog[eng].append(lambda e, fn=fn, h=h, amt=amt: fn(e).then_inc(h, amt))
        self.c[sem] += amt
        return self.c[sem]

    def wait(self, eng, sem, val=None):
        if _STOPPED[0]:
            return
        if val is None:
            val = self.c[sem]
        if val <= 0:
            return
        h = self.sems[sem]
        self.prog[eng].append(lambda e, h=h, val=val: e.wait_ge(h, val))

    def emit(self, block):
        for name in ENGINES:
            prog = self.prog[name]
            def section(eng, prog=prog):
                for fn in prog:
                    fn(eng)
            getattr(block, name)(section)


def build_program():
    _STOPPED[0] = False
    MM_TAGS.clear()
    PHASE[0] = "init"
    nc = bass.Bass("TRN2", target_bir_lowering=False, debug=False)

    x_own = nc.dram_tensor("x_own", [T_CORE, D], F32, kind="ExternalInput").ap()
    x_pre = nc.dram_tensor("x_pre", [T_CORE, D], F32, kind="ExternalInput").ap()
    wqkv = nc.dram_tensor("wqkv", [D, 3 * D], F32, kind="ExternalInput").ap()
    wo = nc.dram_tensor("wo", [D, D], F32, kind="ExternalInput").ap()
    winT = nc.dram_tensor("winT", [D, 4096], F32, kind="ExternalInput").ap()
    w1T = nc.dram_tensor("w1T", [D, 4096], F32, kind="ExternalInput").ap()
    woutT = nc.dram_tensor("woutT", [2048, D], F32, kind="ExternalInput").ap()
    w2T = nc.dram_tensor("w2T", [8 * 2048, D], F32, kind="ExternalInput").ap()
    wgweT = nc.dram_tensor("wgweT", [D, 16], F32, kind="ExternalInput").ap()
    bias_rep = nc.dram_tensor("bias_rep", [128, 16], F32, kind="ExternalInput").ap()
    maskc = nc.dram_tensor("maskc", [128, 128], F32, kind="ExternalInput").ap()
    onehot = nc.dram_tensor("onehot", [8, 8 * 128], F32, kind="ExternalInput").ap()
    ident = nc.dram_tensor("ident", [128, 128], F32, kind="ExternalInput").ap()
    out = nc.dram_tensor("out", [T_CORE, D], F32, kind="ExternalOutput").ap()

    wqkv_bf = nc.dram_tensor("wqkv_bf", [D, 3 * D], BF16, kind="Internal").ap()
    wo_bf = nc.dram_tensor("wo_bf", [D, D], BF16, kind="Internal").ap()
    win_bf = nc.dram_tensor("win_bf", [D, 4096], BF16, kind="Internal").ap()
    w1_bf = nc.dram_tensor("w1_bf", [D, 4096], BF16, kind="Internal").ap()
    wout_bf = nc.dram_tensor("wout_bf", [2048, D], BF16, kind="Internal").ap()
    w2_bf = nc.dram_tensor("w2_bf", [8 * 2048, D], BF16, kind="Internal").ap()

    taps = {}

    with ExitStack() as ctx:
        def sb(name, shape, dt=BF16):
            return ctx.enter_context(nc.sbuf_tensor(name, shape, dt))

        wA = sb("wA", [128, 16384])
        wB = sb("wB", [128, 16384])
        wC = sb("wC", [128, 16384])
        st_sb = sb("st_sb", [128, 1024], F32)
        st_bf = sb("st_bf", [128, 1024])
        mask_sb = sb("mask_sb", [128, 128])
        oneh_sb = sb("oneh_sb", [8, 8 * 128])
        identb = sb("identb", [128, 128])
        wgwe_sb = sb("wgwe_sb", [128, 8 * 16])
        biasr_sb = sb("biasr_sb", [128, 16], F32)
        smalls = sb("smalls", [128, 160], F32)
        wseb = sb("wseb", [128, 32])
        zrow = sb("zrow", [1, 512])
        wse_fm = sb("wse_fm", [8, 512])
        xln = sb("xln", [128, 4 * 1024], F32)
        xn_tm = sb("xn_tm", [128, 2 * 1024])
        xnfm = sb("xnfm", [128, 4096])
        big3 = sb("big3", [128, 12288])
        qfm = big3[:, 0:4096]
        yfm = big3[:, 4096:8192]
        s_tm = big3[:, 8192:12288]
        hbuf = sb("hbuf", [128, 8192])
        ktm = hbuf[:, 0:4096]
        khtm = hbuf[:, 4096:8192]
        vshr = sb("vshr", [128, 4096])
        gsum = hbuf[:, 0:4096]
        vtm = vshr[:, 0:4096]
        shr_tm = vshr[:, 0:4096]
        bc = sb("bc", [128, 8 * 512])
        he = sb("he", [128, 2 * 512])
        scr = sb("scr", [128, 7168])
        khfm = xn_tm[:, 0:1024]
        scoT = scr[:, 0:2048]
        xsq_f = scr[:, 0:2048].bitcast(F32)
        swtmp = [scr[:, 0:1024].bitcast(F32), scr[:, 1024:2048].bitcast(F32)]
        gtmp = [scr[:, 2048:4096].bitcast(F32), scr[:, 4096:6144].bitcast(F32)]
        rsc = scr[:, 5120:7168].bitcast(F32)

        r_lg = rsc[:, 0:16]
        r_d = rsc[:, 16:17]
        r_ad = rsc[:, 17:18]
        r_en = rsc[:, 18:19]
        r_gp = rsc[:, 20:21]
        r_gs = rsc[:, 21:22]
        r_mb0 = rsc[:, 22:23]
        r_mb1 = rsc[:, 23:24]
        r_elm = rsc[:, 24:32]
        r_z = rsc[:, 32:40]
        r_p = rsc[:, 40:48]
        r_eq = rsc[:, 48:56]
        r_pm = rsc[:, 56:64]
        r_keep = rsc[:, 64:72]
        r_m = rsc[:, 72:73]
        r_nm = rsc[:, 73:74]
        r_ss = rsc[:, 74:75]
        r_rs = rsc[:, 75:76]
        r_m1 = rsc[:, 76:77]
        r_m2 = rsc[:, 77:78]

        cpool = sb("cpool", [128, 16], F32)
        sm_bn = [smalls[:, 96 + 12 * c: 108 + 12 * c] for c in range(4)]
        sm_mv = smalls[:, 144:152]
        sm_rs = smalls[:, 152:156]
        sm_lt = smalls[:, 156:160]
        sm_k16 = smalls[:, 16:32]
        sm_kq16 = smalls[:, 32:48]
        sm_kmu = smalls[:, 48:64]
        sm_kvar = smalls[:, 64:80]
        sm_krs = smalls[:, 80:96]

        pa = ctx.enter_context(nc.psum_tensor("pa", [128, 2048], F32))
        pb = ctx.enter_context(nc.psum_tensor("pb", [128, 2048], F32))
        pa_bf = pa[:].bitcast(BF16)
        pb_bf = pb[:].bitcast(BF16)

        semnames = ["sLC", "sCV0", "sCV1", "sCW0", "sCW1", "sWA", "sWB", "sWC",
                    "sLX0", "sLX1", "sLX2", "sLX3", "sST0", "sST1", "sTAP", "sPE", "sV", "sA", "sG"]
        sems = {n: ctx.enter_context(nc.semaphore(n)) for n in semnames}
        block = ctx.enter_context(nc.Block())
        R = Rec(sems)

        def vbar():
            R.wait("vector", "sV")

        def abar():
            R.wait("scalar", "sA")

        def gbar():
            R.wait("gpsimd", "sG")

        _body_guard = True
        CONSTS = [EPS, BIG, -BIG, 1.0, -1.0, 0.0, 1.0 / D, 1.0 / DK, -0.5]
        for i, v in enumerate(CONSTS):
            nc.const_aps.register(F32, v, cpool[:, i:i + 1])

        def tap(name, ap, shape, dt, after_sem):
            if name not in TAPS:
                return
            t_ = nc.dram_tensor("tap_" + name, shape, dt, kind="ExternalOutput").ap()
            taps[name] = t_
            R.wait("gpsimd", after_sem)
            R.inc("gpsimd", lambda e, t_=t_, ap=ap: e.dma_start(out=t_, in_=ap), "sTAP", 16)

        def mm(psum_ap, lhsT, rhs, start, stop, mark=False):
            MM_TAGS.append(PHASE[0])
            fn = lambda e, psum_ap=psum_ap, lhsT=lhsT, rhs=rhs, start=start, stop=stop: \
                e.matmul(psum_ap, lhsT, rhs, start=start, stop=stop)
            if mark:
                return R.inc("tensor", fn, "sPE")
            R.op("tensor", fn)

        R.inc("gpsimd", lambda e: e.dma_start(out=mask_sb[:], in_=maskc), "sLC", 16)
        R.inc("gpsimd", lambda e: e.dma_start(out=oneh_sb[:], in_=onehot), "sLC", 16)
        R.inc("gpsimd", lambda e: e.dma_start(out=identb[:], in_=ident), "sLC", 16)
        R.inc("gpsimd", lambda e: e.dma_start(
            out=wgwe_sb[:].rearrange("p (c n) -> p c n", c=8),
            in_=wgweT.rearrange("(c p) n -> p c n", p=128)), "sLC", 16)
        R.inc("gpsimd", lambda e: e.dma_start(out=biasr_sb[:], in_=bias_rep), "sLC", 16)
        kv_ld = R.inc("gpsimd", lambda e: e.dma_start(
            out=wC[:].rearrange("p (c n) -> p c n", c=8),
            in_=wqkv[:, 1024:3072].rearrange("(c p) n -> p c n", p=128)),
            "sWC", 16)

        conv_jobs = []
        def conv(src, dst, rows, cols):
            max_rows = max(128, (6144 * 128 // cols) // 128 * 128)
            r0 = 0
            while r0 < rows:
                r1 = min(rows, r0 + max_rows)
                nchunk = (r1 - r0) // 128
                conv_jobs.append((
                    src[r0:r1, :].rearrange("(c p) n -> p c n", p=128),
                    dst[r0:r1, :].rearrange("(c p) n -> p c n", p=128),
                    nchunk, cols))
                r0 = r1
            return len(conv_jobs) - 1

        J_KV = conv(wqkv, wqkv_bf, 1024, 3072)
        J_WO = conv(wo, wo_bf, 1024, 1024)
        J_WIN = conv(winT, win_bf, 1024, 4096)
        J_WOUT = conv(woutT, wout_bf, 2048, 1024)
        J_W1 = conv(w1T, w1_bf, 1024, 4096)
        J_W2 = []
        for e_ in range(8):
            J_W2.append(conv(w2T[e_ * 2048:(e_ + 1) * 2048, :],
                             w2_bf[e_ * 2048:(e_ + 1) * 2048, :], 2048, 1024))

        cw_done = {}
        for ji, (src, dst, nchunk, cols) in enumerate(conv_jobs):
            half = ji % 2
            scv, scw = ("sCV0", "sCW0") if half == 0 else ("sCV1", "sCW1")
            cb = big3[:, half * 6144:(half + 1) * 6144]
            cb3 = cb.rearrange("p (c n) -> p c n", c=nchunk)[:, :, 0:cols]
            R.wait("gpsimd", scw)
            R.inc("gpsimd", lambda e, cb3=cb3, src=src: e.dma_start(out=cb3, in_=src),
                  scv, 16)
            R.wait("sync", scv)
            cw_done[ji] = (scw, R.inc(
                "sync", lambda e, dst=dst, cb3=cb3: e.dma_start(out=dst, in_=cb3),
                scw, 16))
        def cw_wait(eng, ji):
            a = max((cw_done[j][1] for j in range(ji + 1) if cw_done[j][0] == "sCW0"),
                    default=0)
            b = max((cw_done[j][1] for j in range(ji + 1) if cw_done[j][0] == "sCW1"),
                    default=0)
            R.wait(eng, "sCW0", a)
            R.wait(eng, "sCW1", b)
        CW_ALL0 = R.c["sCW0"]
        CW_ALL1 = R.c["sCW1"]

        _ckpt("conv")
        R.wait("vector", "sLC", 16 * 5)
        R.inc("vector", lambda e: e.memset(st_sb[:], 0.0), "sV")
        R.inc("vector", lambda e: e.memset(st_bf[:], 0.0), "sV")
        for i, v in enumerate(CONSTS):
            R.inc("vector", lambda e, i=i, v=v: e.memset(cpool[:, i:i + 1], v), "sV")
        R.inc("vector", lambda e: e.memset(zrow[:], 0.0), "sV")

        def ln_stats(c, transp_marks):
            xv = xln[:, c * 1024:(c + 1) * 1024]
            slx = f"sLX{c}"
            bn = sm_bn[c]
            R.wait("vector", slx)
            R.inc("vector", lambda e, xv=xv, bn=bn: e.bn_stats(
                out=bn[:, 0:6], in_=xv[:, 0:512]), "sV")
            R.inc("vector", lambda e, xv=xv, bn=bn: e.bn_stats(
                out=bn[:, 6:12], in_=xv[:, 512:1024]), "sV")
            vbar()
            aggr = R.inc("vector", lambda e, bn=bn, c=c: e.bn_aggr(
                out=sm_mv[:, 2 * c:2 * c + 2], in_=bn), "sV")
            R.wait("scalar", "sV", aggr)
            R.inc("scalar", lambda e, c=c: e.activation(
                out=sm_lt[:, c:c + 1], in_=sm_mv[:, 2 * c + 1:2 * c + 2],
                func=AF.Ln, bias=EPS), "sA")
            abar()
            rs_mark = R.inc("scalar", lambda e, c=c: e.activation(
                out=sm_rs[:, c:c + 1], in_=sm_lt[:, c:c + 1],
                func=AF.Exp, scale=-0.5), "sA")
            return aggr, rs_mark

        def ln_apply(c, aggr, rs_mark, transp_marks, cps):
            half = c % 2
            xv = xln[:, c * 1024:(c + 1) * 1024]
            xnv = xn_tm[:, half * 1024:(half + 1) * 1024]
            R.wait("vector", "sA", rs_mark)
            R.wait("vector", "sPE", transp_marks.get(c - 2, 0))
            ap_mark = R.inc("vector", lambda e, xv=xv, xnv=xnv, c=c: e.tensor_scalar(
                out=xnv, in0=xv, scalar1=sm_mv[:, 2 * c:2 * c + 1],
                scalar2=sm_rs[:, c:c + 1],
                op0=ALU.subtract, op1=ALU.mult), "sV")
            XV_LAST[c] = ap_mark
            bank_bf = pa_bf[:, (c % 2) * 1024:((c % 2) + 1) * 1024]
            R.wait("tensor", "sV", ap_mark)
            for j in range(8):
                fn = lambda e, xnv=xnv, j=j, bank_bf=bank_bf: e.transpose(
                    bank_bf[:, j * 128:(j + 1) * 128], xnv[:, j * 128:(j + 1) * 128],
                    identb[:])
                if j == 7:
                    transp_marks[c] = R.inc("tensor", fn, "sPE")
                else:
                    R.op("tensor", fn)
            dest = xnfm[:].rearrange("p (j t) -> p j t", j=8)[:, :, c * 128:(c + 1) * 128]
            R.wait("vector", "sPE", transp_marks[c])
            cp = R.inc("vector", lambda e, dest=dest, bank_bf=bank_bf: e.tensor_copy(
                out=dest, in_=bank_bf.rearrange("p (j t) -> p j t", j=8)), "sV")
            return ap_mark, cp

        def ln_tile(xsrc, base_tok, loads_fn):
            transp_marks = {}
            apply_marks = {}
            stats = {}
            cps = {}
            loads_fn(0, apply_marks)
            stats[0] = ln_stats(0, transp_marks)
            cp = 0
            for c in range(N_TC):
                if c + 1 < N_TC:
                    loads_fn(c + 1, apply_marks)
                    stats[c + 1] = ln_stats(c + 1, transp_marks)
                ap, cp = ln_apply(c, stats[c][0], stats[c][1], transp_marks, cps)
                apply_marks[c] = ap
                cps[c] = cp
            return cp

        def knorm_chunk(c):
            kin = ktm[:, c * 1024:(c + 1) * 1024]
            kout = khtm[:, c * 1024:(c + 1) * 1024]
            k3 = kin.rearrange("p (h d) -> p h d", h=16)
            R.wait("scalar", "sV")
            R.inc("scalar", lambda e, kin=kin: e.activation(
                out=xsq_f, in_=kin, func=AF.Square), "sA")
            vbar()
            R.inc("vector", lambda e, k3=k3: e.tensor_reduce(
                out=sm_k16, in_=k3, axis=AX.X, op=ALU.add), "sV")
            R.wait("vector", "sA")
            R.inc("vector", lambda e: e.tensor_reduce(
                out=sm_kq16, in_=xsq_f.rearrange("p (h d) -> p h d", h=16),
                axis=AX.X, op=ALU.add), "sV")
            vbar()
            R.inc("vector", lambda e: e.tensor_scalar(
                out=sm_kmu, in0=sm_k16, scalar1=1.0 / DK, scalar2=None, op0=ALU.mult), "sV")
            R.inc("vector", lambda e: e.tensor_scalar(
                out=sm_kvar, in0=sm_kq16, scalar1=1.0 / DK, scalar2=None, op0=ALU.mult), "sV")
            vbar()
            R.inc("vector", lambda e: e.tensor_tensor(
                out=sm_kq16, in0=sm_kmu, in1=sm_kmu, op=ALU.mult), "sV")
            vbar()
            R.inc("vector", lambda e: e.tensor_tensor(
                out=sm_kvar, in0=sm_kvar, in1=sm_kq16, op=ALU.subtract), "sV")
            R.wait("scalar", "sV")
            R.inc("scalar", lambda e: e.activation(
                out=sm_kq16, in_=sm_kvar, func=AF.Ln, bias=EPS), "sA")
            abar()
            R.inc("scalar", lambda e: e.activation(
                out=sm_krs, in_=sm_kq16, func=AF.Exp, scale=-0.5), "sA")
            mu_b = sm_kmu.unsqueeze(2).broadcast_to((128, 16, 64))
            rs_b = sm_krs.unsqueeze(2).broadcast_to((128, 16, 64))
            ko3 = kout.rearrange("p (h d) -> p h d", h=16)
            R.wait("vector", "sA")
            R.inc("vector", lambda e, k3=k3, ko3=ko3, mu_b=mu_b: e.tensor_tensor(
                out=ko3, in0=k3, in1=mu_b, op=ALU.subtract), "sV")
            vbar()
            return R.inc("vector", lambda e, ko3=ko3, rs_b=rs_b: e.tensor_tensor(
                out=ko3, in0=ko3, in1=rs_b, op=ALU.mult), "sV")

        def kv_phase(tile_kv_ld, xnfm_mark, KN, prime_fn=None):
            R.wait("tensor", "sWC", tile_kv_ld)
            R.wait("vector", "sA", SI_LAST[0])
            ev = {}
            last = 0
            for c in range(N_TC):
                for n in range(4):
                    g = c * 4 + n
                    slot = g % 2
                    ps = pa[:, 1024 + slot * 512: 1024 + (slot + 1) * 512]
                    R.wait("tensor", "sV", ev.get(g - 2, xnfm_mark))
                    for k in range(8):
                        mm(ps,
                           xnfm[:, k * 512 + c * 128: k * 512 + (c + 1) * 128],
                           wC[:, k * 2048 + n * 512: k * 2048 + (n + 1) * 512],
                           start=(k == 0), stop=(k == 7), mark=(k == 7))
                    dst = (ktm if n < 2 else vtm)[:, c * 1024 + (n % 2) * 512:
                                                  c * 1024 + (n % 2) * 512 + 512]
                    R.wait("vector", "sPE")
                    last = ev[g] = R.inc("vector", lambda e, dst=dst, ps=ps: e.tensor_copy(
                        out=dst, in_=ps), "sV")
                if c > 1:
                    KN[c - 2] = knorm_chunk(c - 2)
            KN[N_TC - 3] = knorm_chunk(N_TC - 3)
            if prime_fn is not None:
                prime_fn()
            KN[N_TC - 2] = knorm_chunk(N_TC - 2)
            KN[N_TC - 1] = knorm_chunk(N_TC - 1)
            return last

        def u_chunk(c, kn_mark, convert=True):
            ups = pb[:, (c % 2) * 1024:((c % 2) + 1) * 1024]
            R.wait("tensor", "sV", max(kn_mark, UFREE[c % 2]))
            mm(ups[:, 0:512], zrow[0:1, 0:128], zrow[:], start=True, stop=False)
            mm(ups[:, 512:1024], zrow[0:1, 0:128], zrow[:], start=True, stop=False)
            for p in range(NPAIR):
                base = c * 1024 + p * 128
                mm(ups[0:64, p * 128: p * 128 + 64],
                   khtm[:, base: base + 64], vtm[:, base: base + 64],
                   start=False, stop=False, mark=(p == NPAIR - 1))
            R.wait("tensor", "sPE")
            for p in range(NPAIR):
                base = c * 1024 + p * 128
                mm(ups[64:128, p * 128 + 64: p * 128 + 128],
                   khtm[:, base + 64: base + 128], vtm[:, base + 64: base + 128],
                   start=False, stop=True, mark=(p == NPAIR - 1))
            R.wait("vector", "sPE")
            UFREE[c % 2] = R.inc("vector", lambda e, ups=ups: e.tensor_tensor(
                out=st_sb[:], in0=st_sb[:], in1=ups, op=ALU.add), "sV")
            if convert:
                vbar()
                R.inc("vector", lambda e: e.tensor_copy(out=st_bf[:], in_=st_sb[:]), "sV")

        def kht_group(c, wait_sV):
            R.wait("tensor", "sV", wait_sV)
            MM_TAGS.extend(["khT"] * 8)
            km = None
            for p in range(NPAIR):
                fn = lambda e, c=c, p=p: e.transpose(
                    pa_bf[:, p * 128:(p + 1) * 128],
                    khtm[:, c * 1024 + p * 128: c * 1024 + (p + 1) * 128],
                    identb[:])
                if p == NPAIR - 1:
                    km = R.inc("tensor", fn, "sPE")
                else:
                    R.op("tensor", fn)
            R.wait("vector", "sPE", km)
            return R.inc("vector", lambda e: e.tensor_copy(
                out=khfm, in_=pa_bf[:, 0:1024]), "sV")

        PHASE[0] = "prefix"
        ST_CV = 0
        UFREE = [0, 0]
        SI_LAST = {0: 0}
        XV_LAST = {0: 0, 1: 0, 2: 0, 3: 0}
        R.wait("tensor", "sLC", 16 * 5)
        for c in range(N_TC):
            R.inc("scalar", lambda e, c=c: e.dma_start(
                out=xln[:, c * 1024:(c + 1) * 1024],
                in_=x_pre[c * 128:(c + 1) * 128, :]), f"sLX{c}", 16)
        for pt in range(N_PRE):
            def pre_loads(c, apply_marks, pt=pt):
                if pt == 0:
                    return
                xv = xln[:, c * 1024:(c + 1) * 1024]
                R.wait("scalar", "sV", XV_LAST[c])
                R.inc("scalar", lambda e, xv=xv, pt=pt, c=c: e.dma_start(
                    out=xv, in_=x_pre[(pt * 4 + c) * 128:(pt * 4 + c + 1) * 128, :]),
                    f"sLX{c}", 16)
            xnfm_mark = ln_tile(x_pre, pt * 4, pre_loads)
            KN = {}
            kv_phase(kv_ld, xnfm_mark, KN)
            for c in range(N_TC):
                u_chunk(c, KN[c], convert=(pt == N_PRE - 1 and c == N_TC - 1))
        tap("st_prefix", st_sb[:], [128, 1024], F32, "sV")
        ST_CV = R.c["sV"]
        _ckpt("prefix")

        ld_mark = {}
        cw_wait("scalar", J_WO)
        R.inc("scalar", lambda e: e.dma_start(
            out=wB[:, 0:8192].rearrange("p (c n) -> p c n", c=8),
            in_=wqkv_bf[:, 0:1024].rearrange("(c p) n -> p c n", p=128)), "sWB", 16)
        ld_mark["wqwo", 0] = R.inc("scalar", lambda e: e.dma_start(
            out=wB[:, 8192:16384].rearrange("p (c n) -> p c n", c=8),
            in_=wo_bf.rearrange("(c p) n -> p c n", p=128)), "sWB", 16)
        cw_wait("scalar", J_WIN)
        ld_mark["winA", 0] = R.inc("scalar", lambda e: e.dma_start(
            out=wA[:].rearrange("p (c n) -> p c n", c=8),
            in_=win_bf[:, 0:2048].rearrange("(c p) n -> p c n", p=128)), "sWA", 16)
        ld_mark["wkv", 0] = kv_ld

        NEXT_STATS = {}
        LAST_GP_MUL = {}
        for t in range(N_TILES):
            if t == 0:
                def main_loads(c, apply_marks, t=t):
                    xv = xln[:, c * 1024:(c + 1) * 1024]
                    R.wait("scalar", "sV", XV_LAST[c])
                    R.inc("scalar", lambda e, xv=xv, t=t, c=c: e.dma_start(
                        out=xv, in_=x_own[(t * 4 + c) * 128:(t * 4 + c + 1) * 128, :]),
                        f"sLX{c}", 16)
                xnfm_mark = ln_tile(x_own, t * 4, main_loads)
            else:
                transp_marks = {}
                cps = {}
                for c in range(N_TC):
                    _, xnfm_mark = ln_apply(c, NEXT_STATS[c][0], NEXT_STATS[c][1],
                                            transp_marks, cps)
                    cps[c] = xnfm_mark
            if t == 0:
                tap("xnfm", xnfm[:], [128, 4096], BF16, "sV")
                _ckpt("ln0")

            PHASE[0] = "q"
            if t == 0:
                R.wait("vector", "sCW0", CW_ALL0)
                R.wait("vector", "sCW1", CW_ALL1)
                R.wait("tensor", "sLC", 16 * 5)
            R.wait("tensor", "sWB", ld_mark["wqwo", t])
            evq = {}
            for m in range(8):
                ps = pa[:, (m % 2) * 512:((m % 2) + 1) * 512]
                R.wait("tensor", "sV", evq.get(m - 2, xnfm_mark))
                for k in range(8):
                    mm(ps,
                       wB[:, k * 1024 + m * 128: k * 1024 + (m + 1) * 128],
                       xnfm[:, k * 512:(k + 1) * 512],
                       start=(k == 0), stop=(k == 7), mark=(k == 7))
                R.wait("vector", "sPE")
                evq[m] = R.inc("vector", lambda e, m=m, ps=ps: e.tensor_copy(
                    out=qfm[:, m * 512:(m + 1) * 512], in_=ps), "sV")

            PHASE[0] = "kv"
            KN = {}
            PRIME = {}

            def _prime():
                PRIME[0] = kht_group(0, KN[0])
            kv_phase(ld_mark["wkv", t], evq[7], KN, _prime)
            kv_done = R.c["sPE"]
            R.wait("sync", "sPE", kv_done)
            ld_mark["winB", t] = R.inc("sync", lambda e: e.dma_start(
                out=wC[:].rearrange("p (c n) -> p c n", c=8),
                in_=win_bf[:, 2048:4096].rearrange("(c p) n -> p c n", p=128)),
                "sWC", 16)

            if t == 0:
                tap("smalls", smalls[:], [128, 160], F32, "sV")
                tap("xv1", xln[:, 1024:2048], [128, 1024], F32, "sV")
                tap("khtm", khtm[:], [128, 4096], BF16, "sV")
                tap("vtm", vtm[:], [128, 4096], BF16, "sV")
                tap("qfm", qfm[:], [128, 4096], BF16, "sV")
                _ckpt("qkv0")

            PHASE[0] = "scan"
            KE = PRIME[0]
            SE = {}
            YE = {-1: 0}
            CV = {-1: ST_CV}
            MG = {}
            for c in range(N_TC):
                R.wait("tensor", "sV", max(YE[c - 1], KE, CV[c - 1]))
                mm(pb[:, 0:512], zrow[0:1, 0:128], zrow[:], start=True, stop=False)
                mm(pb[:, 512:1024], zrow[0:1, 0:128], zrow[:], start=True, stop=False)
                for p in range(NPAIR):
                    mm(pb[:, p * 128:(p + 1) * 128],
                       st_bf[:, p * 128:(p + 1) * 128],
                       qfm[:, p * 512 + c * 128: p * 512 + (c + 1) * 128],
                       start=False, stop=False)
                for h in range(0, H, 2):
                    p = h // 2
                    mm(pa[:, h * 128:(h + 1) * 128],
                       khfm[0:64, p * 128:(p + 1) * 128],
                       qfm[0:64, p * 512 + c * 128: p * 512 + (c + 1) * 128],
                       start=True, stop=True)
                ups = pb[:, 1024:2048]
                mm(ups[:, 0:512], zrow[0:1, 0:128], zrow[:], start=True, stop=False)
                mm(ups[:, 512:1024], zrow[0:1, 0:128], zrow[:], start=True, stop=False)
                for p in range(NPAIR):
                    base = c * 1024 + p * 128
                    mm(ups[0:64, p * 128: p * 128 + 64],
                       khtm[:, base: base + 64], vtm[:, base: base + 64],
                       start=False, stop=False)
                    mm(ups[64:128, p * 128 + 64: p * 128 + 128],
                       khtm[:, base + 64: base + 128], vtm[:, base + 64: base + 128],
                       start=False, stop=(p == NPAIR - 1), mark=(p == NPAIR - 1))
                mu_ = R.c["sPE"]
                for h in range(1, H, 2):
                    p = h // 2
                    mm(pa[:, h * 128:(h + 1) * 128],
                       khfm[64:128, p * 128:(p + 1) * 128],
                       qfm[64:128, p * 512 + c * 128: p * 512 + (c + 1) * 128],
                       start=True, stop=True, mark=(h == H - 1))
                ms = R.c["sPE"]
                R.wait("vector", "sPE", ms)
                mask3 = mask_sb[:].unsqueeze(1).broadcast_to((128, 16, 128))
                SE[c] = R.inc("vector", lambda e, mask3=mask3: e.tensor_tensor(
                    out=scoT.rearrange("p (h t) -> p h t", h=16),
                    in0=pa[:].rearrange("p (h t) -> p h t", h=16),
                    in1=mask3, op=ALU.mult), "sV")
                MG[c] = R.inc("vector", lambda e, ups=ups: e.tensor_tensor(
                    out=st_sb[:], in0=st_sb[:], in1=ups, op=ALU.add), "sV")
                R.wait("vector", "sV", MG[c])
                CV[c] = R.inc("vector", lambda e: e.tensor_copy(
                    out=st_bf[:], in_=st_sb[:]), "sV")
                if c + 1 < N_TC:
                    KE = kht_group(c + 1, max(SE[c], KN[c + 1]))
                R.wait("tensor", "sV", SE[c])
                for h in range(H):
                    p, u = h // 2, h % 2
                    mm(pb[u * 64:(u + 1) * 64, p * 128:(p + 1) * 128],
                       vtm[:, c * 1024 + p * 128 + u * 64: c * 1024 + p * 128 + u * 64 + 64],
                       scoT[:, h * 128:(h + 1) * 128],
                       start=False, stop=True, mark=(h == H - 1))
                mi = R.c["sPE"]
                ydest = yfm[:].rearrange("p (j t) -> p j t", j=8)[:, :, c * 128:(c + 1) * 128]
                R.wait("vector", "sPE", mi)
                YE[c] = R.inc("vector", lambda e, ydest=ydest: e.tensor_copy(
                    out=ydest, in_=pb[:, 0:1024].rearrange("p (j t) -> p j t", j=8)), "sV")
            ST_CV = CV[N_TC - 1]
            if t == 0:
                tap("yfm", yfm[:], [128, 4096], BF16, "sV")
                tap("st_tile0", st_sb[:], [128, 1024], F32, "sV")
                _ckpt("scan0")

            PHASE[0] = "wo"
            y_mark = R.c["sV"]
            evs = {}
            for c in range(N_TC):
                ps = pa[:, (c % 2) * 1024:((c % 2) + 1) * 1024]
                R.wait("tensor", "sV", evs.get(c - 2, y_mark))
                for p in range(8):
                    for n in range(2):
                        mm(ps[:, n * 512:(n + 1) * 512],
                           yfm[:, p * 512 + c * 128: p * 512 + (c + 1) * 128],
                           wB[:, 8192 + p * 1024 + n * 512: 8192 + p * 1024 + (n + 1) * 512],
                           start=(p == 0), stop=(p == 7), mark=(p == 7 and n == 1))
                R.wait("vector", "sPE")
                evs[c] = R.inc("vector", lambda e, c=c, ps=ps: e.tensor_copy(
                    out=s_tm[:, c * 1024:(c + 1) * 1024], in_=ps), "sV")
            wo_done = R.c["sPE"]
            if t == 0:
                tap("s_tm", s_tm[:], [128, 4096], BF16, "sV")
                _ckpt("wo0")

            PHASE[0] = "router"
            lgB = rsc[:, 0:64]
            lg3 = lgB.rearrange("p (c n) -> p c n", c=4)
            dB = rsc[:, 64:68]
            adB = rsc[:, 68:72]
            enB = rsc[:, 72:76]
            gpB = rsc[:, 76:80]
            gsB = rsc[:, 80:84]
            mb0B = rsc[:, 84:88]
            mb1B = rsc[:, 88:92]
            elm3 = rsc[:, 96:128].rearrange("p (c n) -> p c n", c=4)
            z3 = rsc[:, 128:160].rearrange("p (c n) -> p c n", c=4)
            p3 = rsc[:, 160:192].rearrange("p (c n) -> p c n", c=4)
            eq3 = rsc[:, 192:224].rearrange("p (c n) -> p c n", c=4)
            pm3 = rsc[:, 224:256].rearrange("p (c n) -> p c n", c=4)
            keep3 = rsc[:, 256:288].rearrange("p (c n) -> p c n", c=4)
            tmp3 = rsc[:, 288:320].rearrange("p (c n) -> p c n", c=4)
            mB = rsc[:, 320:324]
            ssB = rsc[:, 324:328]
            rsB = rsc[:, 328:332]
            m1B = rsc[:, 332:336]
            m2B = rsc[:, 336:340]

            def b3(x, w=8):
                return x.unsqueeze(2).broadcast_to((128, 4, w))

            s_mark = R.c["sV"]
            evr = {}
            for c in range(N_TC):
                ps = pb[:, (c % 2) * 512:((c % 2) + 1) * 512]
                R.wait("tensor", "sV", evr.get(c - 2, s_mark))
                for k in range(8):
                    mm(ps[:, 0:16],
                       xnfm[:, k * 512 + c * 128: k * 512 + (c + 1) * 128],
                       wgwe_sb[:, k * 16:(k + 1) * 16],
                       start=(k == 0), stop=(k == 7), mark=(k == 7))
                R.wait("vector", "sPE")
                evr[c] = R.inc("vector", lambda e, ps=ps, c=c: e.tensor_copy(
                    out=lgB[:, c * 16:(c + 1) * 16], in_=ps[:, 0:16]), "sV")
            vbar()
            bias3 = biasr_sb[:].unsqueeze(1).broadcast_to((128, 4, 16))
            R.inc("vector", lambda e: e.tensor_tensor(
                out=lg3, in0=lg3, in1=bias3, op=ALU.add), "sV")
            vbar()
            R.inc("vector", lambda e: e.tensor_tensor(
                out=dB, in0=lg3[:, :, 0:1], in1=lg3[:, :, 1:2], op=ALU.subtract), "sV")
            R.inc("vector", lambda e: e.tensor_tensor(
                out=gsB, in0=lg3[:, :, 1:2], in1=lg3[:, :, 0:1], op=ALU.is_gt), "sV")
            R.wait("scalar", "sV")
            R.inc("scalar", lambda e: e.activation(
                out=adB, in_=dB, func=AF.Abs), "sA")
            abar()
            R.inc("scalar", lambda e: e.activation(
                out=enB, in_=adB, func=AF.Exp, scale=-1.0), "sA")
            R.wait("vector", "sA")
            R.inc("vector", lambda e: e.tensor_scalar(
                out=enB, in0=enB, scalar1=1.0, scalar2=None, op0=ALU.add), "sV")
            vbar()
            R.inc("vector", lambda e: e.reciprocal(out=gpB, in_=enB), "sV")
            R.inc("vector", lambda e: e.tensor_scalar(
                out=mb1B, in0=gsB, scalar1=BIG, scalar2=BIG,
                op0=ALU.mult, op1=ALU.subtract), "sV")
            R.inc("vector", lambda e: e.tensor_scalar(
                out=mb0B, in0=gsB, scalar1=-BIG, scalar2=None, op0=ALU.mult), "sV")
            vbar()
            R.inc("vector", lambda e: e.tensor_tensor(
                out=elm3[:, :, 0:4], in0=lg3[:, :, 2:6], in1=b3(mb0B, 4),
                op=ALU.add), "sV")
            R.inc("vector", lambda e: e.tensor_tensor(
                out=elm3[:, :, 4:8], in0=lg3[:, :, 6:10], in1=b3(mb1B, 4),
                op=ALU.add), "sV")
            vbar()
            R.inc("vector", lambda e: e.tensor_reduce(
                out=mB, in_=elm3, axis=AX.X, op=ALU.max), "sV")
            vbar()
            R.inc("vector", lambda e: e.tensor_tensor(
                out=elm3, in0=elm3, in1=b3(mB), op=ALU.subtract), "sV")
            R.wait("scalar", "sV")
            R.inc("scalar", lambda e: e.activation(
                out=z3, in_=elm3, func=AF.Exp), "sA")
            R.wait("vector", "sA")
            R.inc("vector", lambda e: e.tensor_reduce(
                out=ssB, in_=z3, axis=AX.X, op=ALU.add), "sV")
            vbar()
            R.inc("vector", lambda e: e.reciprocal(out=rsB, in_=ssB), "sV")
            vbar()
            R.inc("vector", lambda e: e.tensor_tensor(
                out=tmp3, in0=z3, in1=b3(rsB), op=ALU.mult), "sV")
            vbar()
            R.inc("vector", lambda e: e.tensor_tensor(
                out=p3, in0=tmp3, in1=b3(gpB), op=ALU.mult), "sV")
            vbar()
            R.inc("vector", lambda e: e.tensor_reduce(
                out=m1B, in_=p3, axis=AX.X, op=ALU.max), "sV")
            vbar()
            R.inc("vector", lambda e: e.tensor_tensor(
                out=eq3, in0=p3, in1=b3(m1B), op=ALU.is_ge), "sV")
            vbar()
            R.inc("vector", lambda e: e.scalar_tensor_tensor(
                out=pm3, in0=eq3, scalar=-BIG, in1=p3,
                op0=ALU.mult, op1=ALU.add), "sV")
            vbar()
            R.inc("vector", lambda e: e.tensor_reduce(
                out=m2B, in_=pm3, axis=AX.X, op=ALU.max), "sV")
            vbar()
            R.inc("vector", lambda e: e.tensor_tensor(
                out=keep3, in0=p3, in1=b3(m2B), op=ALU.is_ge), "sV")
            vbar()
            R.inc("vector", lambda e: e.tensor_tensor(
                out=wseb[:].rearrange("p (c n) -> p c n", c=4),
                in0=p3, in1=keep3, op=ALU.mult), "sV")
            for c in range(N_TC):
                R.wait("tensor", "sV")
                MM_TAGS.append("wseT")
                R.inc("tensor", lambda e, c=c: e.transpose(
                    pb_bf[0:8, 2048:2048 + 128], wseb[:, c * 8:(c + 1) * 8],
                    identb[:]), "sPE")
                R.wait("vector", "sPE")
                R.inc("vector", lambda e, c=c: e.tensor_copy(
                    out=wse_fm[:, c * 128:(c + 1) * 128],
                    in_=pb_bf[0:8, 2048:2048 + 128]), "sV")
            evb = {}
            wse_mark = R.c["sV"]
            for e_ in range(8):
                ps = pb[:, (e_ % 2) * 512:((e_ % 2) + 1) * 512]
                R.wait("tensor", "sV", evb.get(e_ - 2, wse_mark))
                mm(ps, oneh_sb[:, e_ * 128:(e_ + 1) * 128], wse_fm[:],
                   start=True, stop=True, mark=True)
                R.wait("vector", "sPE")
                evb[e_] = R.inc("vector", lambda e, e_=e_, ps=ps: e.tensor_copy(
                    out=bc[:, e_ * 512:(e_ + 1) * 512], in_=ps), "sV")
            router_done = evb[7]
            if t == 0:
                tap("wse_fm", wse_fm[:], [8, 512], BF16, "sV")
                tap("bc", bc[:], [128, 4096], BF16, "sV")
                _ckpt("router0")

            R.wait("sync", "sPE", wo_done)
            cw_wait("sync", J_WOUT)
            ld_mark["wout", t] = R.inc("sync", lambda e: e.dma_start(
                out=wB[:].rearrange("p (c n) -> p c n", c=16),
                in_=wout_bf.rearrange("(c p) n -> p c n", p=128)), "sWB", 16)

            def swiglu_phase(srcA_is_w1, in_ready_sV):
                silu_m = {}
                mult_m = {}
                for halfp in range(2):
                    reg = wA if halfp == 0 else wC
                    key = (("w1A" if halfp == 0 else "w1B") if srcA_is_w1
                           else ("winA" if halfp == 0 else "winB"))
                    R.wait("tensor", "sWA" if halfp == 0 else "sWC", ld_mark[key, t])
                    for pr in range(8):
                        g = halfp * 8 + pr
                        j = g
                        slot = pr % 4
                        pban = pa if slot < 2 else pb
                        pa_a = pban[:, (slot % 2) * 1024: (slot % 2) * 1024 + 512]
                        pa_b = pban[:, (slot % 2) * 1024 + 512: (slot % 2) * 1024 + 1024]
                        R.wait("tensor", "sV", mult_m.get(g - 4, in_ready_sV))
                        R.wait("tensor", "sA", silu_m.get(g - 4, 0))
                        for k in range(8):
                            mm(pa_a,
                               reg[:, k * 2048 + pr * 256: k * 2048 + pr * 256 + 128],
                               xnfm[:, k * 512:(k + 1) * 512],
                               start=(k == 0), stop=(k == 7))
                        for k in range(8):
                            mm(pa_b,
                               reg[:, k * 2048 + pr * 256 + 128: k * 2048 + pr * 256 + 256],
                               xnfm[:, k * 512:(k + 1) * 512],
                               start=(k == 0), stop=(k == 7), mark=(k == 7))
                        sw = swtmp[slot % 2]
                        R.wait("scalar", "sPE")
                        R.wait("scalar", "sV", mult_m.get(g - 2, 0))
                        silu_m[g] = R.inc("scalar", lambda e, sw=sw, pa_a=pa_a: e.activation(
                            out=sw, in_=pa_a, func=AF.Silu), "sA")
                        R.wait("vector", "sA", silu_m[g])
                        mult_m[g] = R.inc(
                            "vector", lambda e, j=j, sw=sw, pa_b=pa_b: e.tensor_tensor(
                                out=hbuf[:, j * 512:(j + 1) * 512], in0=sw, in1=pa_b,
                                op=ALU.mult), "sV")
                    R.wait("sync", "sPE")
                    regsem = "sWA" if halfp == 0 else "sWC"
                    if not srcA_is_w1:
                        cw_wait("sync", J_W1)
                        ld_mark[("w1A" if halfp == 0 else "w1B"), t] = R.inc(
                            "sync", lambda e, reg=reg, halfp=halfp: e.dma_start(
                                out=reg[:].rearrange("p (c n) -> p c n", c=8),
                                in_=w1_bf[:, halfp * 2048:(halfp + 1) * 2048].rearrange(
                                    "(c p) n -> p c n", p=128)), regsem, 16)
                    else:
                        e_ = halfp
                        cw_wait("sync", J_W2[e_])
                        ld_mark["w2", t, e_] = R.inc(
                            "sync", lambda e, reg=reg, e_=e_: e.dma_start(
                                out=reg[:].rearrange("p (c n) -> p c n", c=16),
                                in_=w2_bf[e_ * 2048:(e_ + 1) * 2048, :].rearrange(
                                    "(c p) n -> p c n", p=128)), regsem, 16)
                return mult_m[15], silu_m[15]

            PHASE[0] = "w_in"
            hs_sV, hs_sA = swiglu_phase(False, router_done)
            if t == 0:
                tap("hs", hbuf[:], [128, 8192], BF16, "sV")
                _ckpt("win0")

            PHASE[0] = "w_out"
            R.wait("tensor", "sWB", ld_mark["wout", t])
            evw = {}
            for c in range(N_TC):
                ps = pb[:, (c % 2) * 1024:((c % 2) + 1) * 1024]
                R.wait("tensor", "sV", evw.get(c - 2, hs_sV))
                R.wait("tensor", "sA", hs_sA if c < 2 else 0)
                for j in range(16):
                    for n in range(2):
                        mm(ps[:, n * 512:(n + 1) * 512],
                           hbuf[:, j * 512 + c * 128: j * 512 + (c + 1) * 128],
                           wB[:, j * 1024 + n * 512: j * 1024 + (n + 1) * 512],
                           start=(j == 0), stop=(j == 15), mark=(j == 15 and n == 1))
                R.wait("vector", "sPE")
                evw[c] = R.inc("vector", lambda e, c=c, ps=ps: e.tensor_copy(
                    out=shr_tm[:, c * 1024:(c + 1) * 1024], in_=ps), "sV")
            shr_last = evw[N_TC - 1]
            if t == 0:
                tap("shr_tm", shr_tm[:], [128, 4096], BF16, "sV")
                _ckpt("wout0")

            PHASE[0] = "w1"
            h_sV, h_sA = swiglu_phase(True, shr_last)
            if t == 0:
                tap("h", hbuf[:], [128, 8192], BF16, "sV")
                _ckpt("w1_0")

            PHASE[0] = "routed"
            pregion = [pa[:, 0:1024], pa[:, 1024:2048], pb[:, 0:1024], pb[:, 1024:2048]]
            R.wait("vector", "sV")
            he_marks = []
            for e_ in range(8):
                reg = wA if e_ % 2 == 0 else wC
                R.wait("tensor", "sWA" if e_ % 2 == 0 else "sWC", ld_mark["w2", t, e_])
                for j in range(16):
                    idx = e_ * 16 + j
                    hv = he[:, (idx % 2) * 512:((idx % 2) + 1) * 512]
                    if idx >= 2:
                        R.wait("vector", "sPE", he_marks[idx - 2])
                    else:
                        R.wait("vector", "sPE")
                    R.inc("vector", lambda e, hv=hv, j=j, e_=e_: e.tensor_tensor(
                        out=hv, in0=hbuf[:, j * 512:(j + 1) * 512],
                        in1=bc[:, e_ * 512:(e_ + 1) * 512], op=ALU.mult), "sV")
                    R.wait("tensor", "sV")
                    lastmark = None
                    for c in range(N_TC):
                        for n in range(2):
                            mm(pregion[c][:, n * 512:(n + 1) * 512],
                               hv[:, c * 128:(c + 1) * 128],
                               reg[:, j * 1024 + n * 512: j * 1024 + (n + 1) * 512],
                               start=(e_ == 0 and j == 0),
                               stop=(e_ == 7 and j == 15),
                               mark=(c == N_TC - 1 and n == 1))
                    he_marks.append(R.c["sPE"])
                if e_ + 2 <= 7:
                    R.wait("sync", "sPE")
                    cw_wait("sync", J_W2[e_ + 2])
                    ld_mark["w2", t, e_ + 2] = R.inc(
                        "sync", lambda e, reg=reg, e_=e_: e.dma_start(
                            out=reg[:].rearrange("p (c n) -> p c n", c=16),
                            in_=w2_bf[(e_ + 2) * 2048:(e_ + 3) * 2048, :].rearrange(
                                "(c p) n -> p c n", p=128)),
                        "sWA" if e_ % 2 == 0 else "sWC", 16)
                elif e_ == 6 and t + 1 < N_TILES:
                    R.wait("sync", "sPE")
                    R.inc("sync", lambda e: e.dma_start(
                        out=wB[:, 0:8192].rearrange("p (c n) -> p c n", c=8),
                        in_=wqkv_bf[:, 0:1024].rearrange("(c p) n -> p c n", p=128)),
                        "sWB", 16)
                    ld_mark["wqwo", t + 1] = R.inc("sync", lambda e: e.dma_start(
                        out=wB[:, 8192:16384].rearrange("p (c n) -> p c n", c=8),
                        in_=wo_bf.rearrange("(c p) n -> p c n", p=128)), "sWB", 16)
                    ld_mark["winA", t + 1] = R.inc("sync", lambda e: e.dma_start(
                        out=wA[:].rearrange("p (c n) -> p c n", c=8),
                        in_=win_bf[:, 0:2048].rearrange("(c p) n -> p c n", p=128)),
                        "sWA", 16)
                elif e_ == 7 and t + 1 < N_TILES:
                    R.wait("sync", "sPE")
                    ld_mark["wkv", t + 1] = R.inc("sync", lambda e: e.dma_start(
                        out=wC[:].rearrange("p (c n) -> p c n", c=8),
                        in_=wqkv_bf[:, 1024:3072].rearrange("(c p) n -> p c n", p=128)),
                        "sWC", 16)

            if t == 0:
                _ckpt("routed0")
            GA = {}
            SI = {}
            R.wait("vector", "sA", SI_LAST[0])
            for c in range(N_TC):
                R.wait("vector", "sPE")
                GA[c] = R.inc("vector", lambda e, c=c: e.tensor_tensor(
                    out=gsum[:, c * 1024:(c + 1) * 1024], in0=pregion[c],
                    in1=shr_tm[:, c * 1024:(c + 1) * 1024], op=ALU.add), "sV")
            for c in range(N_TC):
                R.wait("scalar", "sV", GA[c])
                SI[c] = R.inc("scalar", lambda e, c=c: e.activation(
                    out=gsum[:, c * 1024:(c + 1) * 1024],
                    in_=gsum[:, c * 1024:(c + 1) * 1024], func=AF.Silu), "sA")
            SI_LAST[0] = SI[N_TC - 1]
            for c in range(N_TC):
                half = c % 2
                xv = xln[:, c * 1024:(c + 1) * 1024]
                gv = gtmp[half]
                R.wait("gpsimd", "sA", SI[c])
                R.wait("gpsimd", "sST0" if half == 0 else "sST1")
                LAST_GP_MUL[c] = R.inc("gpsimd", lambda e, gv=gv, c=c: e.tensor_tensor(
                    out=gv, in0=gsum[:, c * 1024:(c + 1) * 1024],
                    in1=s_tm[:, c * 1024:(c + 1) * 1024],
                    op=ALU.mult), "sG")
                R.wait("gpsimd", "sG", LAST_GP_MUL[c])
                XV_LAST[c] = R.inc("gpsimd", lambda e, gv=gv, xv=xv: e.tensor_tensor(
                    out=gv, in0=gv, in1=xv, op=ALU.add), "sG")
                R.wait("gpsimd", "sG", XV_LAST[c])
                R.inc("gpsimd", lambda e, gv=gv, t=t, c=c: e.dma_start(
                    out=out[(t * 4 + c) * 128:(t * 4 + c + 1) * 128, :], in_=gv),
                    "sST0" if half == 0 else "sST1", 16)
                if t + 1 < N_TILES:
                    R.inc("gpsimd", lambda e, xv=xv, t=t, c=c: e.dma_start(
                        out=xv, in_=x_own[((t + 1) * 4 + c) * 128:
                                          ((t + 1) * 4 + c + 1) * 128, :]),
                        f"sLX{c}", 16)
            if t + 1 < N_TILES:
                for c in range(N_TC):
                    NEXT_STATS[c] = ln_stats(c, None)

        R.wait("tensor", "sST0")
        R.wait("tensor", "sST1")
        R.wait("tensor", "sTAP")
        R.wait("tensor", "sG")
        R.wait("tensor", "sV")
        R.wait("tensor", "sA")
        R.emit(block)

    return nc, taps


def kernel(**inputs):
    x = np.ascontiguousarray(inputs["L0_attn_x"], dtype=np.float32)
    W_qkv = np.ascontiguousarray(inputs["L0_attn_W_qkv"], dtype=np.float32)
    W_o = np.ascontiguousarray(inputs["L0_attn_W_o"], dtype=np.float32)
    w_in = np.asarray(inputs["L0_ffn_w_in"], dtype=np.float32)
    w_out = np.asarray(inputs["L0_ffn_w_out"], dtype=np.float32)
    w1 = np.asarray(inputs["L0_ffn_w1"], dtype=np.float32)
    w2 = np.asarray(inputs["L0_ffn_w2"], dtype=np.float32)
    wg = np.asarray(inputs["L0_ffn_wg"], dtype=np.float32)
    we = np.asarray(inputs["L0_ffn_we"], dtype=np.float32)
    gb = np.asarray(inputs["L0_ffn_gb"], dtype=np.float32)
    eb = np.asarray(inputs["L0_ffn_eb"], dtype=np.float32)

    def paired(wT):
        a = wT[:, :2048].reshape(1024, 16, 128)
        b = wT[:, 2048:].reshape(1024, 16, 128)
        return np.ascontiguousarray(np.stack([a, b], axis=2).reshape(1024, 4096))

    winT = paired(np.ascontiguousarray(w_in.T))
    w1T = paired(np.ascontiguousarray(w1.T))
    woutT = np.ascontiguousarray(w_out.T)
    w2T = np.ascontiguousarray(w2.transpose(0, 2, 1).reshape(8 * 2048, 1024))
    wgweT = np.zeros((1024, 16), np.float32)
    wgweT[:, 0:2] = wg.T
    wgweT[:, 2:10] = we.T
    bias_rep = np.zeros((128, 16), np.float32)
    bias_rep[:, 0:2] = gb[None, :]
    bias_rep[:, 2:10] = eb[None, :]
    maskc = np.triu(np.ones((128, 128), np.float32))
    onehot = np.zeros((8, 8 * 128), np.float32)
    for e in range(8):
        onehot[e, e * 128:(e + 1) * 128] = 1.0
    identm = np.eye(128, dtype=np.float32)
    zeros_pre = np.zeros((T_CORE, D), np.float32)

    nc, taps = build_program()
    shared = {
        "wqkv": W_qkv, "wo": W_o, "winT": winT, "w1T": w1T, "woutT": woutT,
        "w2T": w2T, "wgweT": wgweT, "bias_rep": bias_rep, "maskc": maskc,
        "onehot": onehot, "ident": identm,
    }
    in_maps = []
    for c in range(8):
        b, half = c // 2, c % 2
        m = dict(shared)
        m["x_own"] = np.ascontiguousarray(x[b, half * T_CORE:(half + 1) * T_CORE, :])
        m["x_pre"] = (zeros_pre if half == 0 else
                      np.ascontiguousarray(x[b, 0:T_CORE, :]))
        in_maps.append(m)

    res = run_bass_kernel_spmd(nc, in_maps, core_ids=list(range(8)))
    outp = np.empty((4, 8192, 1024), np.float32)
    for c in range(8):
        b, half = c // 2, c % 2
        outp[b, half * T_CORE:(half + 1) * T_CORE, :] = res.results[c]["out"]
    kernel.tap_results = res.results
    return outp
